# version 03
# baseline (speedup 1.0000x reference)
import numpy as np
import ml_dtypes
from contextlib import ExitStack

import concourse.bass as bass
import concourse.mybir as mybir
from concourse.bass import IndirectOffsetOnAxis
from concourse.bass_utils import run_bass_kernel_spmd

F32 = mybir.dt.float32
BF16 = mybir.dt.bfloat16
FP8E4 = mybir.dt.float8e4
FP8E5 = mybir.dt.float8e5
I32 = mybir.dt.int32
U32 = mybir.dt.uint32
AX = mybir.AxisListType.X
ALU = mybir.AluOpType
ACT = mybir.ActivationFunctionType

NCORES = 8
T = 2048
D = 1024
NT = T // 128
ND = D // 128
S = 512
NS = T // S
import os
SKIP_LO = os.environ.get('SKIP_LO', '0') == '1'
DEBUG_A = int(os.environ.get('DEBUG_A', '-1'))

ENGINES = ("tensor", "vector", "scalar", "gpsimd", "sync")
DQ = {"sync": "dq_sync", "scalar": "dq_scalar", "gpsimd": "dq_gpsimd"}


class KB:

    def __init__(self, nc, stack):
        self.nc = nc
        self.plan = {e: [] for e in ENGINES}
        self.sem = {}
        for e in ENGINES:
            self.sem[e] = stack.enter_context(nc.semaphore(f"s_{e}"))
        for q in DQ.values():
            self.sem[q] = stack.enter_context(nc.semaphore(f"s_{q}"))
        self.count = {k: 0 for k in self.sem}
        self.waited = {e: {} for e in ENGINES}
        self.last_w = {}
        self.readers = {}
        self.aliases = {}
        self._groups = {}

    def alias(self, group, *keys):
        for k in keys:
            self.aliases[k] = group
            self._groups.setdefault(group, set()).add(k)

    def _conflicts(self, key):
        g = self.aliases.get(key)
        if g is None:
            return (key,)
        return tuple(self._groups[g])

    SELF_WAIT = ("vector", "scalar", "gpsimd")

    def _dep_waits(self, stream, reads, writes):
        need = {}
        def add(owner, val):
            if owner == stream and stream not in self.SELF_WAIT:
                return
            if need.get(owner, 0) < val:
                need[owner] = val
        for key in reads:
            for k in self._conflicts(key):
                lw = self.last_w.get(k)
                if lw:
                    add(*lw)
        for key in writes:
            for k in self._conflicts(key):
                lw = self.last_w.get(k)
                if lw:
                    add(*lw)
                for r in self.readers.get(k, ()):
                    add(*r)
        waits = []
        w = self.waited[stream]
        for owner, val in need.items():
            if w.get(owner, 0) < val:
                waits.append((owner, val))
                w[owner] = val
        return waits

    def emit(self, stream, fn, reads=(), writes=(), dma=False):
        waits = self._dep_waits(stream, reads, writes)
        if dma:
            q = DQ[stream]
            self.count[q] += 16
            owner, val = q, self.count[q]
        else:
            self.count[stream] += 1
            owner, val = stream, self.count[stream]
        self.plan[stream].append((waits, fn, owner, 16 if dma else 1))
        for key in writes:
            self.last_w[key] = (owner, val)
            self.readers[key] = []
        for key in reads:
            self.readers.setdefault(key, []).append((owner, val))
        return owner, val

    def dma(self, stream, out, in_, reads=(), writes=(), **kw):
        return self.emit(stream, lambda e: e.dma_start(out=out, in_=in_, **kw),
                         reads=reads, writes=writes, dma=True)

    def barrier(self):
        snap = {k: self.count[k] for k in self.sem}
        for e in ENGINES:
            waits = []
            w = self.waited[e]
            for o, v in snap.items():
                if o != e and v > 0 and w.get(o, 0) < v:
                    waits.append((o, v))
                    w[o] = v
            if waits:
                self.plan[e].append((waits, None, None, 0))

    def finalize(self, final_waits):
        nc = self.nc
        sems = self.sem

        def runner(stream):
            def run(eng):
                for waits, fn, owner, inc in self.plan[stream]:
                    for o, v in waits:
                        eng.wait_ge(sems[o], v)
                    if fn is not None:
                        inst = fn(eng)
                        inst.then_inc(sems[owner], inc)
                if stream == "sync":
                    for o, v in final_waits:
                        eng.wait_ge(sems[o], v)
            return run

        with nc.Block() as block:
            block.tensor(runner("tensor"))
            block.vector(runner("vector"))
            block.scalar(runner("scalar"))
            block.gpsimd(runner("gpsimd"))
            block.sync(runner("sync"))



def tile_lhsT(wt, kchunks, ochunks):
    K, O = wt.shape
    assert K == kchunks * 128 and O == ochunks * 128
    t = wt.reshape(kchunks, 128, ochunks, 128)
    return np.ascontiguousarray(
        t.transpose(2, 1, 0, 3).reshape(ochunks, 128, kchunks * 128))


def fm_vec(v):
    n = v.shape[0] // 128
    return np.ascontiguousarray(v.reshape(n, 128).T)


ALL_STEPS = ["l0f1", "l0a", "l0f2", "l1f1", "l1a", "l1f2",
             "l2f1", "l2a", "l2f2", "l3f1", "l3a", "l3f2"]


class Prog:
    def __init__(self, steps):
        self.steps = steps
        self.nc = bass.Bass("TRN2", target_bir_lowering=False, debug=False)
        self.stack = ExitStack()
        self.kb = KB(self.nc, self.stack)
        self._psum_rr = 0
        self._sbcache = {}
        self._staged = set()
        self._declare()

    def din(self, name, shape, dtype=F32):
        return self.nc.dram_tensor(name, list(shape), dtype,
                                   kind="ExternalInput").ap()

    def dscr(self, name, shape, dtype=F32):
        return self.nc.dram_tensor(name, list(shape), dtype, kind="Internal").ap()

    def sb(self, name, shape, dtype):
        return self.nc.alloc_sbuf_tensor(name, list(shape), dtype).ap()

    def sb_cached(self, name, shape, dtype):
        if name not in self._sbcache:
            self._sbcache[name] = self.sb(name, shape, dtype)
        return self._sbcache[name]

    MBASE = 18432

    def at(self, name, shape, dtype, off):
        return self.nc.alloc_sbuf_tensor_at(name, list(shape), dtype,
                                            offset=off).ap()

    def _declare(self):
        nc, at = self.nc, self.at
        MB = self.MBASE
        self.x_in = self.din("x_in", (T, D))
        self.x_out = nc.dram_tensor("x_out", [T, D], F32,
                                    kind="ExternalOutput").ap()
        self.x_dram = self.dscr("x_dram", (T, D))
        self.idn_f = self.din("idn_f", (128, 128))
        self.A = [at(f"arena{i}", (128, 32 * S), BF16, MB + i * 32768)
                  for i in range(4)]
        self.fA = at("fA", (128, 16384), F32, MB + 0)
        self.fB = at("fB", (128, 16384), F32, MB + 65536)
        self.kb.alias("ar01", "arena0", "arena1", "fA")
        self.kb.alias("ar23", "arena2", "arena3", "fB")
        self.xh = at("xh", (128, ND * S), BF16, MB + 131072)
        self.xl = at("xl", (128, ND * S), BF16, MB + 139264)
        self.khbuf = at("khbuf", (128, 2048), F32, MB + 131072)
        self.csbuf = at("csbuf", (128, 1024), F32, MB + 139264)
        self.qfbuf = at("qfbuf", (128, 1024), F32, MB + 143360)
        self.kb.alias("u1", "xh", "xl", "khbuf", "csbuf", "qfbuf")
        self.wbuf = [at(f"wbuf{i}", (128, 4096), BF16, MB + 147456 + i * 8192)
                     for i in range(2)]
        self.wbufF = [at(f"wbufF{i}", (128, 2048), F32, MB + 147456 + i * 8192)
                      for i in range(2)]
        self.kb.alias("wb0", "wbuf0", "wbufF0")
        self.kb.alias("wb1", "wbuf1", "wbufF1")
        self.wbuf2 = [at(f"wbuf2_{i}", (128, 1024), BF16, MB + 163840 + i * 2048)
                      for i in range(2)]
        self.wbuf2F = [at(f"wbuf2F{i}", (128, 512), F32, MB + 163840 + i * 2048)
                       for i in range(2)]
        self.kb.alias("w20", "wbuf2_0", "wbuf2F0")
        self.kb.alias("w21", "wbuf2_1", "wbuf2F1")
        self.xt = at("xt", (128, D), F32, MB + 167936)
        self.xt2 = at("xt2", (128, D), F32, MB + 172032)
        self.xnt = at("xnt", (128, D), F32, MB + 176128)
        self.scr1 = at("scr1", (128, 512), F32, MB + 180224)
        self.scr2 = at("scr2", (128, 512), F32, MB + 182272)
        self.sS = at("sS", (128, 2048), F32, MB + 184320)
        self.sP = at("sP", (128, 2048), F32, MB + 192512)
        self.idn = at("idn", (128, 128), F32, MB + 200704)
        self.eps = at("eps", (128, 1), F32, MB + 201216)
        self.statm = at("statm", (128, NT), F32, MB + 201248)
        self.statr = at("statr", (128, NT), F32, MB + 201312)
        self.stat = at("stat", (128, 8), F32, MB + 201376)
        self.fmv1 = at("fmv1", (128, 64), F32, MB + 201408)
        self.fmv2 = at("fmv2", (128, 64), F32, MB + 201664)
        self.tiny = at("tiny", (128, 96), F32, MB + 201920)
        self.ptt = at("ptt", (128, 128), F32, MB + 202304)
        self.bc1 = at("bc1", (128, 1024), F32, MB + 202816)
        self.pows = at("pows", (16, 352), F32, MB + 206912)
        self.f8a = at("f8a", (128, 128), FP8E4, MB + 208320)
        self.f8b = at("f8b", (128, 128), FP8E4, MB + 208448)
        self.f8c = at("f8c", (128, 128), FP8E5, MB + 208576)
        self.idx32 = at("idx32", (128, 8), I32, MB + 208704)
        self.pb = [nc.alloc_psum_tensor(f"pb{i}", [128, 512], F32).ap()
                   for i in range(8)]
        kb = self.kb
        kb.dma("sync", self.idn, self.idn_f, writes=["idn"])
        kb.emit("vector", lambda e: e.memset(self.eps, 1e-5), writes=["eps"])
        kb.dma("sync", self.x_dram, self.x_in, writes=["x_dram"])

    def psum(self, limit=8):
        i = self._psum_rr % limit
        self._psum_rr = (self._psum_rr + 1) % limit
        return self.pb[i], f"pb{i}"

    def emit_ln_stats(self, src_tile_fn=None):
        kb, stat = self.kb, self.stat
        for g in range(NT):
            xt = self.xt
            if src_tile_fn is None:
                kb.dma("sync", xt, self.x_dram[g * 128:(g + 1) * 128, :],
                       reads=["x_dram"], writes=["xt"])
            else:
                src_tile_fn(g)
            kb.emit("vector", (lambda xt: lambda e: e.reduce_sum(
                self.stat[:, 0:1], xt, AX))(xt), reads=["xt"], writes=["stat"])
            kb.emit("scalar", (lambda xt: lambda e: e.activation(
                self.scr1, xt[:, 0:512], ACT.Square, accum_out=self.stat[:, 2:3]))(xt),
                    reads=["xt"], writes=["stat", "scr1"])
            kb.emit("scalar", (lambda xt: lambda e: e.activation(
                self.scr1, xt[:, 512:1024], ACT.Square, accum_out=self.stat[:, 3:4]))(xt),
                    reads=["xt"], writes=["stat", "scr1"])
            kb.emit("vector", (lambda g: lambda e: e.tensor_scalar(
                self.statm[:, g:g + 1], self.stat[:, 0:1], 1.0 / D, None,
                ALU.mult))(g), reads=["stat"], writes=["statm"])
            kb.emit("vector", lambda e: e.tensor_add(
                self.stat[:, 4:5], self.stat[:, 2:3], self.stat[:, 3:4]),
                    reads=["stat"], writes=["stat"])
            kb.emit("vector", (lambda g: lambda e: e.tensor_tensor(
                self.stat[:, 5:6], self.statm[:, g:g + 1], self.statm[:, g:g + 1],
                ALU.mult))(g), reads=["statm"], writes=["stat"])
            kb.emit("vector", lambda e: e.tensor_scalar(
                self.stat[:, 6:7], self.stat[:, 4:5], 1.0 / D, None, ALU.mult),
                    reads=["stat"], writes=["stat"])
            kb.emit("vector", lambda e: e.tensor_sub(
                self.stat[:, 4:5], self.stat[:, 6:7], self.stat[:, 5:6]),
                    reads=["stat"], writes=["stat"])
            kb.emit("scalar", lambda e: e.activation(
                self.stat[:, 7:8], self.stat[:, 4:5], ACT.Sqrt,
                bias=self.eps), reads=["stat", "eps"], writes=["stat"])
            kb.emit("vector", (lambda g: lambda e: e.reciprocal(
                self.statr[:, g:g + 1], self.stat[:, 7:8]))(g),
                    reads=["stat"], writes=["statr"])

    def emit_xn_stripe(self, s, xh, xl, keys=("xh", "xl"), S_=S):
        kb = self.kb
        ntile = S_ // 128
        for gi in range(ntile):
            g = (s * S_) // 128 + gi
            xt, xnt = self.xt, self.xnt
            kb.dma("sync", xt, self.x_dram[g * 128:(g + 1) * 128, :],
                   reads=["x_dram"], writes=["xt"])
            kb.emit("vector", (lambda xt, xnt, g: lambda e: e.tensor_scalar(
                xnt, xt, self.statm[:, g:g + 1], self.statr[:, g:g + 1],
                ALU.subtract, ALU.mult))(xt, xnt, g),
                    reads=["xt", "statm", "statr"], writes=["xnt"])
            for c in range(ND):
                ps, pk = self.psum()
                pv = ps[:, 0:128]
                kb.emit("tensor", (lambda pv, c: lambda e: e.transpose(
                    pv, self.xnt[:, c * 128:(c + 1) * 128], self.idn))(pv, c),
                        reads=["xnt", "idn"], writes=[pk])
                hd = xh[:, c * S_ + gi * 128: c * S_ + (gi + 1) * 128]
                ld = xl[:, c * S_ + gi * 128: c * S_ + (gi + 1) * 128]
                kb.emit("scalar", (lambda hd, pv: lambda e: e.copy(hd, pv))(hd, pv),
                        reads=[pk], writes=[keys[0]])
                kb.emit("vector", (lambda pv, hd: lambda e: e.tensor_sub(
                    self.scr2[:, 0:128], pv, hd))(pv, hd),
                        reads=[pk, keys[0]], writes=["scr2"])
                kb.emit("vector", (lambda ld: lambda e: e.tensor_copy(
                    ld, self.scr2[:, 0:128]))(ld),
                        reads=["scr2"], writes=[keys[1]])

    def emit_resid(self, psum_ap, psum_key, g, n0, width, scale, last,
                   extra_reads=()):
        kb = self.kb
        xsl = self.xt2[:, 0:width]
        src = self.x_dram[g * 128:(g + 1) * 128, n0:n0 + width]
        dst = (self.x_out if last else self.x_dram)[g * 128:(g + 1) * 128,
                                                    n0:n0 + width]
        kb.dma("sync", xsl, src, reads=["x_dram"], writes=["xt2"])
        res = self.xnt[:, 0:width]
        kb.emit("vector",
                (lambda res, xsl, psum_ap, scale: lambda e: e.scalar_tensor_tensor(
                    res, psum_ap, scale, xsl, ALU.mult, ALU.add))(res, xsl, psum_ap, scale),
                reads=[psum_key, "xt2"] + list(extra_reads), writes=["xnt"])
        kb.dma("sync", dst, res, reads=["xnt"],
               writes=["x_out"] if last else ["x_dram"])

    def decl_tffn(self, L):
        d = {}
        d["w1"] = self.din(f"t{L}_w1", (32, 128, 1024))
        d["w2"] = self.din(f"t{L}_w2", (32, 128, 1024))
        d["h1"] = self.din(f"t{L}_h1", (32, 128, 4096))
        d["h2"] = self.din(f"t{L}_h2", (32, 128, 4096))
        d["w3"] = self.din(f"t{L}_w3", (32, 128, 1024))
        d["s12"] = self.din(f"t{L}_s12", (128, 64))
        d["sh12"] = self.din(f"t{L}_sh12", (128, 64))
        d["s3"] = self.din(f"t{L}_s3", (1024,))
        for k in ("w1", "w2", "h1", "h2", "w3"):
            d[k + "b"] = self.dscr(f"t{L}_{k}b", d[k].tensor.shape, BF16)
        d["L"] = L
        return d

    def stage_tffn(self, d):
        kb = self.kb
        for k in ("w1", "w2", "h1", "h2", "w3"):
            src, dst = d[k], d[k + "b"]
            n = src.tensor.shape[0]
            for i in range(0, n, 8):
                kb.dma("gpsimd", dst[i:i + 8], src[i:i + 8],
                       writes=[f"tstage{d['L']}"])

    def emit_tffn(self, d, scale, last=False):
        kb = self.kb
        L = d["L"]
        s12, sh12, s3bc = self.fmv1, self.fmv2, self.bc1
        kb.dma("sync", s12, d["s12"], writes=["fmv1"])
        kb.dma("sync", sh12, d["sh12"], writes=["fmv2"])
        kb.dma("sync", s3bc, d["s3"].unsqueeze(0).broadcast_to((128, 1024)),
               writes=["bc1"])
        self.emit_ln_stats()
        ah, al, bh, bl = self.A
        AK = ["arena0", "arena1", "arena2", "arena3"]
        wb, wb2 = self.wbuf, self.wbuf2
        tstg = f"tstage{L}"
        for s in range(NS):
            self.emit_xn_stripe(s, self.xh, self.xl)
            if DEBUG_A == 0 and s == 0:
                for c in range(4):
                    kb.emit("vector", (lambda c: lambda e: e.tensor_copy(
                        self.scr1, self.xh[:, c * S:c * S + 512]))(c),
                            reads=["xh"], writes=["scr1"])
                    kb.emit("vector", (lambda c: lambda e: e.tensor_copy(
                        self.scr2, self.xl[:, c * S:c * S + 512]))(c),
                            reads=["xl", "scr1"], writes=["scr2"])
                    kb.emit("vector", lambda e: e.tensor_add(
                        self.scr1, self.scr1, self.scr2),
                            reads=["scr1", "scr2"], writes=["scr1"])
                    kb.dma("sync", self.x_out[c * 128:(c + 1) * 128, 0:512],
                           self.scr1, reads=["scr1"], writes=["x_out"])
            for m in range(32):
                k1 = f"wbuf{m % 2}"
                k2 = f"wbuf2_{m % 2}"
                b1, b2 = wb[m % 2], wb2[m % 2]
                kb.dma("scalar", b1[:, 0:1024], d["w1b"][m],
                       reads=[tstg], writes=[k1])
                kb.dma("gpsimd", b2[:, 0:1024], d["w2b"][m],
                       reads=[tstg], writes=[k2])
                pA, pkA = self.psum()
                pB, pkB = self.psum()
                for pk_, buf_, key_ in ((pkA, b1, k1), (pkB, b2, k2)):
                    pp = pA if pk_ == pkA else pB
                    for k in range(ND):
                        for pi, xs in (((0, self.xh),) if SKIP_LO else ((0, self.xh), (1, self.xl))):
                            kb.emit("tensor", (lambda pp, lt, rhs, st, sp:
                                    lambda e: e.matmul(pp, lt, rhs, start=st,
                                                       stop=sp))(
                                        pp, buf_[:, k * 128:(k + 1) * 128],
                                        xs[:, k * S:(k + 1) * S],
                                        (k == 0 and pi == 0),
                                        (k == ND - 1 and pi == (0 if SKIP_LO else 1))),
                                    reads=[key_, "xh", "xl"], writes=[pk_])
                kb.emit("vector", (lambda pA, m: lambda e: e.tensor_scalar(
                    self.scr1, pA, s12[:, m:m + 1], None, ALU.mult))(pA, m),
                        reads=[pkA, "fmv1"], writes=["scr1"])
                kb.emit("scalar", lambda e: e.activation(
                    self.scr1, self.scr1, ACT.Silu),
                        reads=["scr1"], writes=["scr1"])
                kb.emit("vector", (lambda pB, m: lambda e: e.scalar_tensor_tensor(
                    self.scr2, pB, s12[:, 32 + m:33 + m], self.scr1,
                    ALU.mult, ALU.mult))(pB, m),
                        reads=[pkB, "scr1", "fmv1"], writes=["scr2"])
                ahm = ah[:, m * S:(m + 1) * S]
                alm = al[:, m * S:(m + 1) * S]
                kb.emit("scalar", (lambda ahm: lambda e: e.copy(
                    ahm, self.scr2))(ahm), reads=["scr2"], writes=[AK[0]])
                kb.emit("vector", (lambda ahm: lambda e: e.tensor_sub(
                    self.scr1, self.scr2, ahm))(ahm),
                        reads=["scr2", AK[0]], writes=["scr1"])
                kb.emit("vector", (lambda alm: lambda e: e.tensor_copy(
                    alm, self.scr1))(alm), reads=["scr1"], writes=[AK[1]])
                if DEBUG_A == 1 and s == 0 and m < 4:
                    kb.dma("sync", self.x_out[m * 128:(m + 1) * 128, 0:512],
                           self.scr2, reads=["scr2"], writes=["x_out"])
            for stage, (sh_, sl_, dh_, dl_, wkey, scol, shk, slk, dhk, dlk) in \
                enumerate(((ah, al, bh, bl, "h1b", 0, AK[0], AK[1], AK[2], AK[3]),
                           (bh, bl, ah, al, "h2b", 32, AK[2], AK[3], AK[0], AK[1]))):
                for m in range(32):
                    k1 = f"wbuf{m % 2}"
                    b1 = wb[m % 2]
                    kb.dma("scalar", b1, d[wkey][m], reads=[tstg], writes=[k1])
                    pA, pkA = self.psum()
                    for k in range(32):
                        for pi, xs in (((0, sh_),) if SKIP_LO else ((0, sh_), (1, sl_))):
                            kb.emit("tensor", (lambda pA, lt, rhs, st, sp:
                                    lambda e: e.matmul(pA, lt, rhs, start=st,
                                                       stop=sp))(
                                        pA, b1[:, k * 128:(k + 1) * 128],
                                        xs[:, k * S:(k + 1) * S],
                                        (k == 0 and pi == 0),
                                        (k == 31 and pi == (0 if SKIP_LO else 1))),
                                    reads=[k1, shk, slk], writes=[pkA])
                    kb.emit("vector", (lambda pA, m, scol: lambda e: e.tensor_scalar(
                        self.scr1, pA, sh12[:, scol + m:scol + m + 1], None,
                        ALU.mult))(pA, m, scol),
                            reads=[pkA, "fmv2"], writes=["scr1"])
                    kb.emit("scalar", lambda e: e.activation(
                        self.scr1, self.scr1, ACT.Gelu),
                            reads=["scr1"], writes=["scr1"])
                    dhm = dh_[:, m * S:(m + 1) * S]
                    dlm = dl_[:, m * S:(m + 1) * S]
                    kb.emit("scalar", (lambda dhm: lambda e: e.copy(
                        dhm, self.scr1))(dhm), reads=["scr1"], writes=[dhk])
                    kb.emit("vector", (lambda dhm: lambda e: e.tensor_sub(
                        self.scr2, self.scr1, dhm))(dhm),
                            reads=["scr1", dhk], writes=["scr2"])
                    kb.emit("vector", (lambda dlm: lambda e: e.tensor_copy(
                        dlm, self.scr2))(dlm), reads=["scr2"], writes=[dlk])
            for n in range(2):
                pbs = [self.psum() for _ in range(4)]
                for k in range(32):
                    kw = f"wbuf{k % 2}"
                    bw = wb[k % 2]
                    kb.dma("scalar", bw[:, 0:512],
                           d["w3b"][k][:, n * 512:(n + 1) * 512],
                           reads=[tstg], writes=[kw])
                    for gi in range(4):
                        pA, pkA = pbs[gi]
                        for pi, xs in (((0, ah),) if SKIP_LO else ((0, ah), (1, al))):
                            kb.emit("tensor", (lambda pA, lt, rhs, st, sp:
                                    lambda e: e.matmul(pA, lt, rhs, start=st,
                                                       stop=sp))(
                                        pA, xs[:, k * S + gi * 128:
                                               k * S + (gi + 1) * 128],
                                        bw[:, 0:512],
                                        (k == 0 and pi == 0),
                                        (k == 31 and pi == (0 if SKIP_LO else 1))),
                                    reads=[kw, AK[0], AK[1]], writes=[pkA])
                for gi in range(4):
                    pA, pkA = pbs[gi]
                    g = s * (S // 128) + gi
                    kb.emit("vector", (lambda pA, n: lambda e: e.tensor_mul(
                        self.scr1, pA, s3bc[:, n * 512:(n + 1) * 512]))(pA, n),
                            reads=[pkA, "bc1"], writes=["scr1"])
                    self.emit_resid(self.scr1, "scr1", g, n * 512, 512, scale,
                                    last)

    def emit_ln_full(self, dst, dkey, gamma_ap=None, beta_ap=None, gb_key=None,
                     src_tile_fn=None):
        kb = self.kb
        self.emit_ln_stats(src_tile_fn)
        for g in range(NT):
            if src_tile_fn is None:
                kb.dma("sync", self.xt, self.x_dram[g * 128:(g + 1) * 128, :],
                       reads=["x_dram"], writes=["xt"])
            else:
                src_tile_fn(g)
            kb.emit("vector", (lambda g: lambda e: e.tensor_scalar(
                self.xnt, self.xt, self.statm[:, g:g + 1],
                self.statr[:, g:g + 1], ALU.subtract, ALU.mult))(g),
                    reads=["xt", "statm", "statr"], writes=["xnt"])
            for c in range(ND):
                ps, pk = self.psum()
                pv = ps[:, 0:128]
                kb.emit("tensor", (lambda pv, c: lambda e: e.transpose(
                    pv, self.xnt[:, c * 128:(c + 1) * 128], self.idn))(pv, c),
                        reads=["xnt", "idn"], writes=[pk])
                dd = dst[:, c * T + g * 128: c * T + (g + 1) * 128]
                if gamma_ap is None:
                    kb.emit("vector", (lambda dd, pv: lambda e: e.tensor_copy(
                        dd, pv))(dd, pv), reads=[pk], writes=[dkey])
                else:
                    kb.emit("vector", (lambda dd, pv, c: lambda e: e.tensor_scalar(
                        dd, pv, gamma_ap[:, c:c + 1], beta_ap[:, c:c + 1],
                        ALU.mult, ALU.add))(dd, pv, c),
                            reads=[pk, gb_key], writes=[dkey])

    def proj_tm(self, src_fm, src_key, rhs_dram, kchunks, ocols, evac,
                dma_q="scalar", Ttok=T):
        kb = self.kb
        ntile = Ttok // 128
        nn = (ocols + 511) // 512
        for ggrp in range(0, ntile, 8):
            gs = list(range(ggrp, min(ggrp + 8, ntile)))
            for n in range(nn):
                w = min(512, ocols - n * 512)
                pbs = [self.psum() for _ in gs]
                for k in range(kchunks):
                    kw = "wb%d" % (k % 2)
                    bw = self.wbufF[k % 2][:, 0:w]
                    kb.dma(dma_q, bw, rhs_dram[k][:, n * 512:n * 512 + w],
                           writes=[kw])
                    for gi, g in enumerate(gs):
                        pA, pkA = pbs[gi]
                        po = pA[:, 0:w] if w < 512 else pA
                        lt = src_fm[:, k * Ttok + g * 128: k * Ttok + (g + 1) * 128]
                        kb.emit("tensor", (lambda po, lt, bw, st, sp:
                                lambda e: e.matmul(po, lt, bw, start=st,
                                                   stop=sp))(po, lt, bw,
                                    (k == 0), (k == kchunks - 1)),
                                reads=[kw, src_key], writes=[pkA])
                for gi, g in enumerate(gs):
                    pA, pkA = pbs[gi]
                    evac(pA[:, 0:w] if w < 512 else pA, pkA, g, n)

    def proj_fm(self, src_fm, src_key, lhsT_dram, kchunks, ochunks, dst, dkey,
                post, dma_q="scalar", Ttok=T):
        kb = self.kb
        nstr = Ttok // 512
        for m in range(ochunks):
            kw = "wb%d" % (m % 2)
            bw = self.wbufF[m % 2][:, 0:kchunks * 128] if kchunks <= 16 else None
            kb.dma(dma_q, bw, lhsT_dram[m], writes=[kw])
            for si in range(nstr):
                pA, pkA = self.psum()
                for k in range(kchunks):
                    lt = bw[:, k * 128:(k + 1) * 128]
                    rhs = src_fm[:, k * Ttok + si * 512: k * Ttok + (si + 1) * 512]
                    kb.emit("tensor", (lambda pA, lt, rhs, st, sp:
                            lambda e: e.matmul(pA, lt, rhs, start=st, stop=sp))(
                                pA, lt, rhs, (k == 0), (k == kchunks - 1)),
                            reads=[kw, src_key], writes=[pkA])
                dd = dst[:, m * Ttok + si * 512: m * Ttok + (si + 1) * 512]
                post(pA, pkA, m, si, dd)

    def emit_qd(self, tile_ap, tile_key, kind, nblk):
        kb = self.kb
        maxval = 448.0 if kind == "e4m3fn" else 57344.0
        for b in range(nblk):
            sl = tile_ap[:, b * 128:(b + 1) * 128]
            am = self.tiny[:, 0:1]
            sc = self.tiny[:, 1:2]
            z = self.scr1[:, 0:128]
            kb.emit("vector", (lambda am, sl: lambda e: e.reduce_max(
                am, sl, AX, apply_absolute_value=True))(am, sl),
                    reads=[tile_key], writes=["tiny"])
            kb.emit("vector", (lambda sc, am: lambda e: e.tensor_scalar(
                sc, am, 1.0 / maxval, None, ALU.mult))(sc, am),
                    reads=["tiny"], writes=["tiny"])
            rsc = self.tiny[:, 6:7]
            kb.emit("vector", (lambda rsc, sc: lambda e: e.reciprocal(
                rsc, sc))(rsc, sc), reads=["tiny"], writes=["tiny"])
            kb.emit("vector", (lambda z, sl, rsc: lambda e: e.tensor_scalar(
                z, sl, rsc, None, ALU.mult))(z, sl, rsc),
                    reads=[tile_key, "tiny"], writes=["scr1"])
            if kind == "e5m2":
                zb = self.scr2[:, 0:128]
                kb.emit("vector", (lambda z: lambda e: e.tensor_copy(
                    self.f8c, z))(z), reads=["scr1"], writes=["f8c"])
                kb.emit("vector", (lambda zb: lambda e: e.tensor_copy(
                    zb, self.f8c))(zb), reads=["f8c"], writes=["scr2"])
                kb.emit("vector", (lambda sl, zb, sc: lambda e: e.tensor_scalar(
                    sl, zb, sc, None, ALU.mult))(sl, zb, sc),
                        reads=["scr2", "tiny"], writes=[tile_key])
            else:
                zh = self.scr1[:, 128:256]
                za = self.scr2[:, 0:128]
                zb = self.scr2[:, 128:256]
                mk = self.scr1[:, 256:384]
                zq = self.scr1[:, 384:512]
                kb.emit("vector", (lambda zh, z: lambda e: e.tensor_scalar(
                    zh, z, 0.5, None, ALU.mult))(zh, z),
                        reads=["scr1"], writes=["scr1"])
                kb.emit("vector", (lambda zh: lambda e: e.tensor_copy(
                    self.f8a, zh))(zh), reads=["scr1"], writes=["f8a"])
                kb.emit("vector", (lambda za: lambda e: e.tensor_copy(
                    za, self.f8a))(za), reads=["f8a"], writes=["scr2"])
                zcl = self.scr2[:, 256:384]
                kb.emit("vector", (lambda zcl, z: lambda e: e.tensor_scalar(
                    zcl, z, -1.0, 1.0, ALU.max, ALU.min))(zcl, z),
                        reads=["scr1"], writes=["scr2"])
                kb.emit("vector", (lambda zcl: lambda e: e.tensor_copy(
                    self.f8b, zcl))(zcl), reads=["scr2"], writes=["f8b"])
                kb.emit("vector", (lambda zb: lambda e: e.tensor_copy(
                    zb, self.f8b))(zb), reads=["f8b"], writes=["scr2"])
                ab = self.scr2[:, 384:512]
                kb.emit("scalar", (lambda ab, z: lambda e: e.activation(
                    ab, z, ACT.Abs))(ab, z), reads=["scr1"], writes=["scr2"])
                kb.emit("vector", (lambda mk, ab: lambda e: e.tensor_scalar(
                    mk, ab, 1.0, None, ALU.is_ge))(mk, ab),
                        reads=["scr2"], writes=["scr1"])
                kb.emit("vector", (lambda zh, za: lambda e: e.tensor_add(
                    zh, za, za))(zh, za), reads=["scr2"], writes=["scr1"])
                kb.emit("vector", (lambda za, zh, zb: lambda e: e.tensor_sub(
                    za, zh, zb))(za, zh, zb), reads=["scr1", "scr2"],
                        writes=["scr2"])
                kb.emit("vector", (lambda zh, za, mk: lambda e: e.tensor_mul(
                    zh, za, mk))(zh, za, mk), reads=["scr2", "scr1"],
                        writes=["scr1"])
                kb.emit("vector", (lambda zq, zh, zb: lambda e: e.tensor_add(
                    zq, zh, zb))(zq, zh, zb), reads=["scr1", "scr2"],
                        writes=["scr1"])
                kb.emit("vector", (lambda sl, zq, sc: lambda e: e.tensor_scalar(
                    sl, zq, sc, None, ALU.mult))(sl, zq, sc),
                        reads=["scr1", "tiny"], writes=[tile_key])

    def emit_tm_to_fm(self, src_tile, src_key, dst_fm, dst_key, g, nchunks=ND,
                      Ttok=T):
        kb = self.kb
        for c in range(nchunks):
            ps, pk = self.psum()
            pv = ps[:, 0:128]
            kb.emit("tensor", (lambda pv, c: lambda e: e.transpose(
                pv, src_tile[:, c * 128:(c + 1) * 128], self.idn))(pv, c),
                    reads=[src_key, "idn"], writes=[pk])
            dd = dst_fm[:, c * Ttok + g * 128: c * Ttok + (g + 1) * 128]
            kb.emit("vector", (lambda dd, pv: lambda e: e.tensor_copy(
                dd, pv))(dd, pv), reads=[pk], writes=[dst_key])

    def stage_quant_weights_e4m3(self, w_nat, wT_dram, otiles, kchunks):
        kb = self.kb
        for i in range(otiles):
            kb.dma("sync", self.xt[:, 0:kchunks * 128], w_nat[i],
                   writes=["xt"])
            self.emit_qd(self.xt[:, 0:kchunks * 128], "xt", "e4m3fn", kchunks)
            for c in range(kchunks):
                ps, pk = self.psum()
                pv = ps[:, 0:128]
                kb.emit("tensor", (lambda pv, c: lambda e: e.transpose(
                    pv, self.xt[:, c * 128:(c + 1) * 128], self.idn))(pv, c),
                        reads=["xt", "idn"], writes=[pk])
                kb.emit("vector", (lambda pv: lambda e: e.tensor_copy(
                    self.ptt, pv))(pv), reads=[pk], writes=["ptt"])
                kb.dma("sync", wT_dram[c][:, i * 128:(i + 1) * 128], self.ptt,
                       reads=["ptt"], writes=["wT"])

    def decl_l1a(self):
        d = {}
        for nm in ("wq", "wk", "wv", "wo"):
            d[nm] = self.din(f"l1_{nm}", (8, 128, 1024))
            d[nm + "T"] = self.dscr(f"l1_{nm}T", (8, 128, 1024))
        d["lnw"] = self.din("l1_lnw", (1024,))
        d["lnb"] = self.din("l1_lnb", (1024,))
        d["cos"] = self.din("l1_cos", (2048, 512))
        d["sin"] = self.din("l1_sin", (2048, 512))
        d["q"] = self.dscr("l1_q", (2048, 1024))
        d["k"] = self.dscr("l1_k", (2048, 1024))
        d["v"] = self.dscr("l1_v", (2048, 1024))
        return d

    def emit_l1a(self, d, last=False):
        kb = self.kb
        for nm in ("wq", "wk", "wv", "wo"):
            self.stage_quant_weights_e4m3(d[nm], d[nm + "T"], 8, 8)
        kb.barrier()
        def src_xd(g):
            kb.dma("sync", self.xt, self.x_dram[g * 128:(g + 1) * 128, :],
                   reads=["x_dram"], writes=["xt"])
            self.emit_qd(self.xt, "xt", "e4m3fn", ND)
        self.emit_ln_stats(src_xd)
        kb.dma("sync", self.bc1, d["lnw"].unsqueeze(0).broadcast_to((128, 1024)),
               writes=["bc1"])
        kb.dma("sync", self.sP[:, 0:1024],
               d["lnb"].unsqueeze(0).broadcast_to((128, 1024)), writes=["sP"])
        for g in range(NT):
            src_xd(g)
            kb.emit("vector", (lambda g: lambda e: e.tensor_scalar(
                self.xnt, self.xt, self.statm[:, g:g + 1],
                self.statr[:, g:g + 1], ALU.subtract, ALU.mult))(g),
                    reads=["xt", "statm", "statr"], writes=["xnt"])
            kb.emit("vector", lambda e: e.tensor_mul(
                self.xt, self.xnt, self.bc1), reads=["xnt", "bc1"],
                    writes=["xt"])
            kb.emit("vector", lambda e: e.tensor_add(
                self.xnt, self.xt, self.sP[:, 0:1024]),
                    reads=["xt", "sP"], writes=["xnt"])
            self.emit_qd(self.xnt, "xnt", "e4m3fn", ND)
            self.emit_tm_to_fm(self.xnt, "xnt", self.fA, "fA", g)
        for nm, dst in (("wq", d["q"]), ("wk", d["k"]), ("wv", d["v"])):
            def evac(pA, pkA, g, n, dst=dst):
                kb.emit("vector", (lambda pA: lambda e: e.tensor_copy(
                    self.scr1, pA))(pA), reads=[pkA], writes=["scr1"])
                kb.dma("sync", dst[g * 128:(g + 1) * 128, n * 512:(n + 1) * 512],
                       self.scr1, reads=["scr1"], writes=["qkv"])
            self.proj_tm(self.fA, "fA", d[nm + "T"], ND, 1024, evac)
        for nm, to_fm in (("q", False), ("k", True)):
            src = d[nm]
            for g in range(NT):
                kb.dma("sync", self.xt, src[g * 128:(g + 1) * 128, :],
                       reads=["qkv"], writes=["xt"])
                kb.dma("scalar", self.csbuf[:, 0:512],
                       d["cos"][g * 128:(g + 1) * 128, :], writes=["csbuf"])
                kb.dma("gpsimd", self.csbuf[:, 512:1024],
                       d["sin"][g * 128:(g + 1) * 128, :], writes=["csbuf"])
                x1 = self.xt[:, 0:512]
                x2 = self.xt[:, 512:1024]
                co = self.csbuf[:, 0:512]
                si = self.csbuf[:, 512:1024]
                t1 = self.scr1
                t2 = self.scr2
                kb.emit("vector", lambda e: e.tensor_mul(t1, x1, co),
                        reads=["xt", "csbuf"], writes=["scr1"])
                kb.emit("vector", lambda e: e.tensor_mul(t2, x2, si),
                        reads=["xt", "csbuf"], writes=["scr2"])
                kb.emit("vector", lambda e: e.tensor_sub(
                    self.xnt[:, 0:512], t1, t2), reads=["scr1", "scr2"],
                        writes=["xnt"])
                kb.emit("vector", lambda e: e.tensor_mul(t1, x1, si),
                        reads=["xt", "csbuf"], writes=["scr1"])
                kb.emit("vector", lambda e: e.tensor_mul(t2, x2, co),
                        reads=["xt", "csbuf"], writes=["scr2"])
                kb.emit("vector", lambda e: e.tensor_add(
                    self.xnt[:, 512:1024], t1, t2), reads=["scr1", "scr2"],
                        writes=["xnt"])
                self.emit_qd(self.xnt, "xnt", "e5m2", ND)
                if to_fm:
                    self.emit_tm_to_fm(self.xnt, "xnt", self.fB, "fB", g)
                else:
                    kb.dma("sync", src[g * 128:(g + 1) * 128, :], self.xnt,
                           reads=["xnt"], writes=["qkv"])
        DL1 = int(os.environ.get("DL1", "0"))
        if DL1 == 1:
            kb.dma("sync", self.x_out, d["q"], reads=["qkv"], writes=["x_out"])
            return
        if DL1 == 2:
            kb.dma("sync", self.x_out, d["v"], reads=["qkv"], writes=["x_out"])
            return
        for g in range(NT):
            kb.dma("sync", self.xt, d["q"][g * 128:(g + 1) * 128, :],
                   reads=["qkv"], writes=["xt"])
            self.emit_tm_to_fm(self.xt, "xt", self.qfbuf, "qfbuf", 0, ND, 128)
            pbs = [self.psum(6) for _ in range(4)]
            for c in range(ND):
                lt = self.qfbuf[:, c * 128:(c + 1) * 128]
                for n in range(4):
                    pA, pkA = pbs[n]
                    rhs = self.fB[:, c * T + n * 512:(c * T + (n + 1) * 512)]
                    kb.emit("tensor", (lambda pA, lt, rhs, st, sp:
                            lambda e: e.matmul(pA, lt, rhs, start=st, stop=sp))(
                                pA, lt, rhs, (c == 0), (c == ND - 1)),
                            reads=["qfbuf", "fB"], writes=[pkA])
            for n in range(4):
                pA, pkA = pbs[n]
                kb.emit("vector", (lambda pA, n: lambda e: e.tensor_copy(
                    self.sS[:, n * 512:(n + 1) * 512], pA))(pA, n),
                        reads=[pkA], writes=["sS"])
            m = self.tiny[:, 2:3]
            nb = self.tiny[:, 3:4]
            dn = self.tiny[:, 4:5]
            rdn = self.tiny[:, 5:6]
            kb.emit("vector", lambda e: e.reduce_max(m, self.sS, AX),
                    reads=["sS"], writes=["tiny"])
            kb.emit("vector", lambda e: e.tensor_scalar(
                nb, m, -1.0 / 32.0, None, ALU.mult), reads=["tiny"],
                    writes=["tiny"])
            kb.emit("scalar", lambda e: e.activation(
                self.sP, self.sS, ACT.Exp, bias=nb, scale=1.0 / 32.0,
                accum_out=dn), reads=["sS", "tiny"], writes=["sP", "tiny"])
            kb.emit("vector", lambda e: e.reciprocal(rdn, dn),
                    reads=["tiny"], writes=["tiny"])
            po0, pko0 = self.pb[6], "pb6"
            po1, pko1 = self.pb[7], "pb7"
            for sc in range(NT):
                ps, pk = self.psum(6)
                pv = ps[:, 0:128]
                kb.emit("tensor", (lambda pv, sc: lambda e: e.transpose(
                    pv, self.sP[:, sc * 128:(sc + 1) * 128], self.idn))(pv, sc),
                        reads=["sP", "idn"], writes=[pk])
                kb.emit("vector", (lambda pv: lambda e: e.tensor_copy(
                    self.ptt, pv))(pv), reads=[pk], writes=["ptt"])
                for n, (po, pko) in enumerate(((po0, pko0), (po1, pko1))):
                    kw = "wb%d" % (n % 2)
                    bw = self.wbufF[n % 2][:, 0:512]
                    kb.dma("scalar", bw,
                           d["v"][sc * 128:(sc + 1) * 128, n * 512:(n + 1) * 512],
                           reads=["qkv"], writes=[kw])
                    kb.emit("tensor", (lambda po, bw, st, sp:
                            lambda e: e.matmul(po, self.ptt, bw, start=st,
                                               stop=sp))(po, bw, (sc == 0),
                                                         (sc == NT - 1)),
                            reads=["ptt", kw], writes=[pko])
            for n, (po, pko) in enumerate(((po0, pko0), (po1, pko1))):
                kb.emit("vector", (lambda po, n: lambda e: e.tensor_scalar(
                    self.xnt[:, n * 512:(n + 1) * 512], po, rdn, None,
                    ALU.mult))(po, n), reads=[pko, "tiny"], writes=["xnt"])
            self.emit_qd(self.xnt, "xnt", "e4m3fn", ND)
            self.emit_tm_to_fm(self.xnt, "xnt", self.fA, "fA", g)
        def evac_o(pA, pkA, g, n):
            self.emit_resid(pA, pkA, g, n * 512, 512, 1.0, last)
        self.proj_tm(self.fA, "fA", d["woT"], ND, 1024, evac_o)

    def decl_l0a(self):
        d = {}
        d["wqkv"] = self.din("l0_wqkv", (8, 128, 1536))
        d["wo"] = self.din("l0_wo", (8, 128, 1024))
        d["cos"] = self.din("l0_cos", (2048, 32))
        d["sin"] = self.din("l0_sin", (2048, 32))
        d["mask4"] = self.din("l0_mask4", (4, 128, 512))
        d["qkv"] = self.dscr("l0_qkv", (2048, 1536))
        d["qh"] = self.dscr("l0_qh", (16, 64, 2048))
        d["kh"] = self.dscr("l0_kh", (4, 64, 2048))
        d["ao"] = self.dscr("l0_ao", (2048, 1024))
        return d

    def _l2norm_rope(self, nheads, base, d, g):
        kb = self.kb
        kb.dma("scalar", self.csbuf[:, 0:32],
               d["cos"][g * 128:(g + 1) * 128, :], writes=["csbuf"])
        kb.dma("gpsimd", self.csbuf[:, 32:64],
               d["sin"][g * 128:(g + 1) * 128, :], writes=["csbuf"])
        for h in range(nheads):
            sl = self.xt[:, base + h * 64: base + (h + 1) * 64]
            ssq = self.tiny[:, 0:1]
            kb.emit("scalar", (lambda sl, ssq: lambda e: e.activation(
                self.scr1[:, 0:64], sl, ACT.Square, accum_out=ssq))(sl, ssq),
                    reads=["xt"], writes=["scr1", "tiny"])
            kb.emit("scalar", lambda e: e.activation(
                self.tiny[:, 1:2], self.tiny[:, 0:1], ACT.Sqrt),
                    reads=["tiny"], writes=["tiny"])
            kb.emit("vector", lambda e: e.tensor_scalar(
                self.tiny[:, 2:3], self.tiny[:, 1:2], 1e-10, None, ALU.add),
                    reads=["tiny"], writes=["tiny"])
            kb.emit("vector", lambda e: e.reciprocal(
                self.tiny[:, 3:4], self.tiny[:, 2:3]),
                    reads=["tiny"], writes=["tiny"])
            nsl = self.xnt[:, 0:64]
            kb.emit("vector", (lambda nsl, sl: lambda e: e.tensor_scalar(
                nsl, sl, self.tiny[:, 3:4], None, ALU.mult))(nsl, sl),
                    reads=["xt", "tiny"], writes=["xnt"])
            x1, x2 = nsl[:, 0:32], nsl[:, 32:64]
            co, si = self.csbuf[:, 0:32], self.csbuf[:, 32:64]
            t1, t2 = self.scr1[:, 64:96], self.scr1[:, 96:128]
            kb.emit("vector", (lambda t1, x1: lambda e: e.tensor_mul(
                t1, x1, co))(t1, x1), reads=["xnt", "csbuf"], writes=["scr1"])
            kb.emit("vector", (lambda t2, x2: lambda e: e.tensor_mul(
                t2, x2, si))(t2, x2), reads=["xnt", "csbuf"], writes=["scr1"])
            kb.emit("vector", (lambda sl, t1, t2: lambda e: e.tensor_sub(
                sl[:, 0:32], t1, t2))(sl, t1, t2),
                    reads=["scr1"], writes=["xt"])
            kb.emit("vector", (lambda t1, x2: lambda e: e.tensor_mul(
                t1, x2, co))(t1, x2), reads=["xnt", "csbuf"], writes=["scr1"])
            kb.emit("vector", (lambda t2, x1: lambda e: e.tensor_mul(
                t2, x1, si))(t2, x1), reads=["xnt", "csbuf"], writes=["scr1"])
            kb.emit("vector", (lambda sl, t1, t2: lambda e: e.tensor_add(
                sl[:, 32:64], t2, t1))(sl, t1, t2),
                    reads=["scr1"], writes=["xt"])

    def emit_l0a(self, d, last=False):
        kb = self.kb
        self.emit_ln_full(self.fA, "fA")
        def evac(pA, pkA, g, n):
            kb.emit("vector", (lambda pA: lambda e: e.tensor_copy(
                self.scr1, pA))(pA), reads=[pkA], writes=["scr1"])
            kb.dma("sync", d["qkv"][g * 128:(g + 1) * 128,
                                    n * 512:(n + 1) * 512],
                   self.scr1, reads=["scr1"], writes=["qkv"])
        self.proj_tm(self.fA, "fA", d["wqkv"], ND, 1536, evac)
        for g in range(NT):
            kb.dma("sync", self.xt, d["qkv"][g * 128:(g + 1) * 128, 0:1024],
                   reads=["qkv"], writes=["xt"])
            self._l2norm_rope(16, 0, d, g)
            for h in range(16):
                ps, pk = self.psum()
                pv = ps[0:64, 0:128]
                kb.emit("tensor", (lambda pv, h: lambda e: e.transpose(
                    pv, self.xt[:, h * 64:(h + 1) * 64], self.idn))(pv, h),
                        reads=["xt", "idn"], writes=[pk])
                kb.emit("vector", (lambda pv: lambda e: e.tensor_copy(
                    self.ptt[0:64, :], pv))(pv), reads=[pk], writes=["ptt"])
                kb.dma("sync", d["qh"][h][:, g * 128:(g + 1) * 128],
                       self.ptt[0:64, :], reads=["ptt"], writes=["qh"])
            kb.dma("sync", self.xt[:, 0:512],
                   d["qkv"][g * 128:(g + 1) * 128, 1024:1536],
                   reads=["qkv"], writes=["xt"])
            self._l2norm_rope(4, 0, d, g)
            for h in range(4):
                ps, pk = self.psum()
                pv = ps[0:64, 0:128]
                kb.emit("tensor", (lambda pv, h: lambda e: e.transpose(
                    pv, self.xt[:, h * 64:(h + 1) * 64], self.idn))(pv, h),
                        reads=["xt", "idn"], writes=[pk])
                kb.emit("vector", (lambda pv: lambda e: e.tensor_copy(
                    self.ptt[0:64, :], pv))(pv), reads=[pk], writes=["ptt"])
                kb.dma("sync", d["kh"][h][:, g * 128:(g + 1) * 128],
                       self.ptt[0:64, :], reads=["ptt"], writes=["kh"])
            dd = self.fA[:, g * 256:(g + 1) * 256]
            kb.emit("vector", (lambda dd: lambda e: e.tensor_copy(
                dd, self.xt[:, 256:512]))(dd), reads=["xt"], writes=["fA"])
        for h in range(16):
            kb.dma("sync", self.khbuf[0:64, :], d["kh"][h // 4],
                   reads=["kh"], writes=["khbuf"])
            for g in range(NT):
                nk = ((g + 1) * 128 + 511) // 512
                kb.dma("sync", self.qfbuf[0:64, 0:128],
                       d["qh"][h][:, g * 128:(g + 1) * 128],
                       reads=["qh"], writes=["qfbuf"])
                vno = (g * 128) - (nk - 1) * 512
                kb.dma("scalar", self.csbuf[:, 0:512],
                       d["mask4"][vno // 128], writes=["csbuf"])
                for n in range(nk):
                    pA, pkA = self.psum(6)
                    kb.emit("tensor", (lambda pA, n: lambda e: e.matmul(
                        pA, self.qfbuf[0:64, 0:128],
                        self.khbuf[0:64, n * 512:(n + 1) * 512],
                        start=True, stop=True))(pA, n),
                            reads=["qfbuf", "khbuf"], writes=[pkA])
                    if n == nk - 1:
                        kb.emit("vector", (lambda pA, n: lambda e: e.tensor_add(
                            self.sS[:, n * 512:(n + 1) * 512], pA,
                            self.csbuf[:, 0:512]))(pA, n),
                                reads=[pkA, "csbuf"], writes=["sS"])
                    else:
                        kb.emit("vector", (lambda pA, n: lambda e: e.tensor_copy(
                            self.sS[:, n * 512:(n + 1) * 512], pA))(pA, n),
                                reads=[pkA], writes=["sS"])
                m = self.tiny[:, 2:3]
                nb = self.tiny[:, 3:4]
                dn = self.tiny[:, 4:5]
                rdn = self.tiny[:, 5:6]
                sSv = self.sS[:, 0:nk * 512]
                sPv = self.sP[:, 0:nk * 512]
                kb.emit("vector", (lambda sSv: lambda e: e.reduce_max(
                    m, sSv, AX))(sSv), reads=["sS"], writes=["tiny"])
                kb.emit("vector", lambda e: e.tensor_scalar(
                    nb, m, -0.125, None, ALU.mult), reads=["tiny"],
                        writes=["tiny"])
                kb.emit("scalar", (lambda sPv, sSv: lambda e: e.activation(
                    sPv, sSv, ACT.Exp, bias=nb, scale=0.125,
                    accum_out=dn))(sPv, sSv),
                        reads=["sS", "tiny"], writes=["sP", "tiny"])
                kb.emit("vector", lambda e: e.reciprocal(rdn, dn),
                        reads=["tiny"], writes=["tiny"])
                po, pko = self.pb[7], "pb7"
                for sc in range(g + 1):
                    ps, pk = self.psum(6)
                    pv = ps[:, 0:128]
                    kb.emit("tensor", (lambda pv, sc: lambda e: e.transpose(
                        pv, self.sP[:, sc * 128:(sc + 1) * 128], self.idn))(pv, sc),
                            reads=["sP", "idn"], writes=[pk])
                    kb.emit("vector", (lambda pv: lambda e: e.tensor_copy(
                        self.ptt, pv))(pv), reads=[pk], writes=["ptt"])
                    rhs = self.fA[:, sc * 256 + (h // 4) * 64:
                                  sc * 256 + (h // 4) * 64 + 64]
                    kb.emit("tensor", (lambda po, rhs, st, sp: lambda e: e.matmul(
                        po[:, 0:64], self.ptt, rhs, start=st, stop=sp))(
                            po, rhs, (sc == 0), (sc == g)),
                            reads=["ptt", "fA"], writes=[pko])
                kb.emit("vector", (lambda po: lambda e: e.tensor_scalar(
                    self.scr1[:, 0:64], po[:, 0:64], rdn, None, ALU.mult))(po),
                        reads=[pko, "tiny"], writes=["scr1"])
                kb.dma("sync", d["ao"][g * 128:(g + 1) * 128,
                                       h * 64:(h + 1) * 64],
                       self.scr1[:, 0:64], reads=["scr1"], writes=["ao"])
        for g in range(NT):
            kb.dma("sync", self.xt, d["ao"][g * 128:(g + 1) * 128, :],
                   reads=["ao"], writes=["xt"])
            self.emit_tm_to_fm(self.xt, "xt", self.fB, "fB", g)
        def evac_o(pA, pkA, g, n):
            self.emit_resid(pA, pkA, g, n * 512, 512, 1.0, last)
        self.proj_tm(self.fB, "fB", d["wo"], ND, 1024, evac_o)

    def decl_l2a(self):
        d = {}
        d["wq"] = self.din("l2_wq", (8, 128, 1024))
        d["wk"] = self.din("l2_wk", (4, 128, 1024))
        d["wv"] = self.din("l2_wv", (4, 128, 1024))
        d["wo"] = self.din("l2_wo", (8, 128, 1024))
        for nm, oc in (("wq", 8), ("wk", 4), ("wv", 4), ("wo", 8)):
            d[nm + "T"] = self.dscr(f"l2_{nm}T", (8, 128, oc * 128))
        d["bq"] = self.din("l2_bq", (1024,))
        d["bk"] = self.din("l2_bk", (512,))
        d["bv"] = self.din("l2_bv", (512,))
        d["bo"] = self.din("l2_bo", (1024,))
        d["lng"] = self.din("l2_lng", (1024,))
        d["lnb"] = self.din("l2_lnb", (1024,))
        d["q"] = self.dscr("l2_q", (2048, 1024))
        d["k"] = self.dscr("l2_k", (2048, 512))
        d["v"] = self.dscr("l2_v", (2048, 512))
        d["qh"] = self.dscr("l2_qh", (16, 64, 2048))
        d["kh"] = self.dscr("l2_kh", (8, 64, 2048))
        d["ao"] = self.dscr("l2_ao", (2048, 1024))
        d["sscr"] = self.dscr("l2_sscr", (8,))
        return d

    def _round_clip(self, dst, src, tmp, dst_key, src_key, tmp_key, lo, hi):
        kb = self.kb
        C = 12582912.0
        kb.emit("vector", (lambda tmp, src: lambda e: e.tensor_scalar(
            tmp, src, C, C, ALU.add, ALU.subtract))(tmp, src),
                reads=[src_key], writes=[tmp_key])
        kb.emit("vector", (lambda dst, tmp: lambda e: e.tensor_scalar(
            dst, tmp, lo, hi, ALU.max, ALU.min))(dst, tmp),
                reads=[tmp_key], writes=[dst_key])

    def stage_bitlin_w(self, w_nat, wT_dram, otiles, sidx, d):
        kb = self.kb
        acc = self.tiny[:, 8:9]
        one = self.tiny[:, 9:10]
        kb.emit("vector", lambda e: e.memset(acc, 0.0), writes=["tiny"])
        kb.emit("vector", lambda e: e.memset(one, 1.0), writes=["tiny"])
        for i in range(otiles):
            kb.dma("sync", self.xt, w_nat[i], writes=["xt"])
            kb.emit("vector", lambda e: e.tensor_reduce(
                self.tiny[:, 10:11], self.xt, AX, ALU.add,
                apply_absolute_value=True), reads=["xt"], writes=["tiny"])
            kb.emit("vector", lambda e: e.tensor_add(
                acc, acc, self.tiny[:, 10:11]), reads=["tiny"], writes=["tiny"])
        ps, pk = self.psum()
        kb.emit("tensor", lambda e: e.matmul(ps[0:1, 0:1], acc, one,
                                             start=True, stop=True),
                reads=["tiny"], writes=[pk])
        tot = self.tiny[0:1, 11:12]
        kb.emit("vector", (lambda ps: lambda e: e.tensor_scalar(
            tot, ps[0:1, 0:1], 1.0 / (otiles * 128 * 1024), None,
            ALU.mult))(ps), reads=[pk], writes=["tiny"])
        tot2 = self.tiny[0:1, 16:17]
        kb.emit("vector", lambda e: e.tensor_scalar(
            tot2, tot, 1e-5, None, ALU.max), reads=["tiny"], writes=["tiny"])
        kb.dma("sync", d["sscr"][sidx:sidx + 1].unsqueeze(0), tot2,
               reads=["tiny"], writes=["sscr"])
        swb = self.tiny[:, 12:13]
        kb.dma("sync", swb, d["sscr"][sidx:sidx + 1].unsqueeze(0)
               .broadcast_to((128, 1)), reads=["sscr"], writes=["tiny"])
        rsw = self.tiny[:, 13:14]
        kb.emit("vector", lambda e: e.reciprocal(rsw, swb),
                reads=["tiny"], writes=["tiny"])
        for i in range(otiles):
            kb.dma("sync", self.xt, w_nat[i], writes=["xt"])
            kb.emit("vector", lambda e: e.tensor_scalar(
                self.xnt, self.xt, rsw, None, ALU.mult),
                    reads=["xt", "tiny"], writes=["xnt"])
            self._round_clip(self.xnt, self.xnt, self.xt, "xnt", "xnt", "xt",
                             -1.0, 1.0)
            for c in range(ND):
                ps, pk = self.psum()
                pv = ps[:, 0:128]
                kb.emit("tensor", (lambda pv, c: lambda e: e.transpose(
                    pv, self.xnt[:, c * 128:(c + 1) * 128], self.idn))(pv, c),
                        reads=["xnt", "idn"], writes=[pk])
                kb.emit("vector", (lambda pv: lambda e: e.tensor_copy(
                    self.ptt, pv))(pv), reads=[pk], writes=["ptt"])
                kb.dma("sync", wT_dram[c][:, i * 128:(i + 1) * 128], self.ptt,
                       reads=["ptt"], writes=["wT"])

    def _rms_actquant_tile(self, g, rsx_col):
        kb = self.kb
        kb.emit("scalar", lambda e: e.activation(
            self.scr1, self.xnt[:, 0:512], ACT.Square,
            accum_out=self.tiny[:, 0:1]), reads=["xnt"], writes=["scr1", "tiny"])
        kb.emit("scalar", lambda e: e.activation(
            self.scr1, self.xnt[:, 512:1024], ACT.Square,
            accum_out=self.tiny[:, 1:2]), reads=["xnt"], writes=["scr1", "tiny"])
        kb.emit("vector", lambda e: e.tensor_add(
            self.tiny[:, 2:3], self.tiny[:, 0:1], self.tiny[:, 1:2]),
                reads=["tiny"], writes=["tiny"])
        kb.emit("vector", lambda e: e.tensor_scalar(
            self.tiny[:, 3:4], self.tiny[:, 2:3], 1.0 / D, None, ALU.mult),
                reads=["tiny"], writes=["tiny"])
        kb.emit("scalar", lambda e: e.activation(
            self.tiny[:, 4:5], self.tiny[:, 3:4], ACT.Sqrt, bias=self.eps),
                reads=["tiny", "eps"], writes=["tiny"])
        kb.emit("vector", lambda e: e.reciprocal(
            self.tiny[:, 5:6], self.tiny[:, 4:5]), reads=["tiny"],
                writes=["tiny"])
        kb.emit("vector", lambda e: e.tensor_scalar(
            self.xt, self.xnt, self.tiny[:, 5:6], None, ALU.mult),
                reads=["xnt", "tiny"], writes=["xt"])
        kb.emit("vector", lambda e: e.reduce_max(
            self.tiny[:, 6:7], self.xt, AX, apply_absolute_value=True),
                reads=["xt"], writes=["tiny"])
        kb.emit("vector", lambda e: e.tensor_scalar(
            self.tiny[:, 17:18], self.tiny[:, 6:7], 1e-5, None, ALU.max),
                reads=["tiny"], writes=["tiny"])
        kb.emit("vector", (lambda rsx_col: lambda e: e.tensor_scalar(
            rsx_col, self.tiny[:, 17:18], 1.0 / 127.0, None, ALU.mult))(rsx_col),
                reads=["tiny"], writes=["fmv1"])
        kb.emit("vector", lambda e: e.reciprocal(
            self.tiny[:, 7:8], self.tiny[:, 17:18]), reads=["tiny"],
                writes=["tiny"])
        kb.emit("vector", lambda e: e.tensor_scalar(
            self.xnt, self.xt, self.tiny[:, 7:8], 127.0, ALU.mult, ALU.mult),
                reads=["xt", "tiny"], writes=["xnt"])
        self._round_clip(self.xnt, self.xnt, self.xt, "xnt", "xnt", "xt",
                         -128.0, 127.0)

    def emit_l2a(self, d, last=False):
        kb = self.kb
        for nm, ot, si in (("wq", 8, 0), ("wk", 4, 1), ("wv", 4, 2),
                           ("wo", 8, 3)):
            self.stage_bitlin_w(d[nm], d[nm + "T"], ot, si, d)
        kb.barrier()
        self.emit_ln_stats()
        for g in range(NT):
            kb.dma("sync", self.xt, self.x_dram[g * 128:(g + 1) * 128, :],
                   reads=["x_dram"], writes=["xt"])
            kb.emit("vector", (lambda g: lambda e: e.tensor_scalar(
                self.xnt, self.xt, self.statm[:, g:g + 1],
                self.statr[:, g:g + 1], ALU.subtract, ALU.mult))(g),
                    reads=["xt", "statm", "statr"], writes=["xnt"])
            self._rms_actquant_tile(g, self.fmv1[:, g:g + 1])
            self.emit_tm_to_fm(self.xnt, "xnt", self.fA, "fA", g)
        sws = self.tiny[:, 14:15]
        for nm, oc, si, dst, bnm in (("wq", 8, 0, d["q"], "bq"),
                                     ("wk", 4, 1, d["k"], "bk"),
                                     ("wv", 4, 2, d["v"], "bv")):
            kb.dma("sync", sws, d["sscr"][si:si + 1].unsqueeze(0)
                   .broadcast_to((128, 1)), reads=["sscr"], writes=["tiny"])
            kb.dma("sync", self.bc1[:, 0:oc * 128],
                   d[bnm].unsqueeze(0).broadcast_to((128, oc * 128)),
                   writes=["bc1"])
            def evac(pA, pkA, g, n, dst=dst, oc=oc):
                kb.emit("vector", (lambda pA, g: lambda e: e.tensor_scalar(
                    self.scr1, pA, self.fmv1[:, g:g + 1], self.tiny[:, 14:15],
                    ALU.mult, ALU.mult))(pA, g),
                        reads=[pkA, "fmv1", "tiny"], writes=["scr1"])
                kb.emit("vector", (lambda n: lambda e: e.tensor_add(
                    self.scr1, self.scr1, self.bc1[:, n * 512:(n + 1) * 512]))(n),
                        reads=["scr1", "bc1"], writes=["scr1"])
                kb.dma("sync", dst[g * 128:(g + 1) * 128,
                                   n * 512:(n + 1) * 512], self.scr1,
                       reads=["scr1"], writes=["qkv2"])
            self.proj_tm(self.fA, "fA", d[nm + "T"], ND, oc * 128, evac)
        DL2 = int(os.environ.get("DL2", "0"))
        if DL2 == 3:
            kb.dma("sync", self.x_out[0:1, 0:8], d["sscr"].unsqueeze(0),
                   reads=["sscr"], writes=["x_out"])
            kb.dma("sync", self.x_out[1:9, 0:1024], d["wqT"][0][0:8, :],
                   reads=["wT"], writes=["x_out"])
            return
        if DL2 == 4:
            for c in range(8):
                kb.dma("sync", self.x_out[c * 128:(c + 1) * 128, 0:1024],
                       d["wqT"][c], reads=["wT"], writes=["x_out"])
            return
        if DL2 == 5:
            for i in range(16):
                kb.dma("sync", self.x_out[i * 128:(i + 1) * 128, 0:1024],
                       self.fA[:, i * 1024:(i + 1) * 1024], reads=["fA"],
                       writes=["x_out"])
            return
        if DL2 == 1:
            kb.dma("sync", self.x_out, d["q"], reads=["qkv2"], writes=["x_out"])
            return
        for g in range(NT):
            kb.dma("sync", self.xt, d["q"][g * 128:(g + 1) * 128, :],
                   reads=["qkv2"], writes=["xt"])
            for h in range(16):
                ps, pk = self.psum()
                pv = ps[0:64, 0:128]
                kb.emit("tensor", (lambda pv, h: lambda e: e.transpose(
                    pv, self.xt[:, h * 64:(h + 1) * 64], self.idn))(pv, h),
                        reads=["xt", "idn"], writes=[pk])
                kb.emit("vector", (lambda pv: lambda e: e.tensor_copy(
                    self.ptt[0:64, :], pv))(pv), reads=[pk], writes=["ptt"])
                kb.dma("sync", d["qh"][h][:, g * 128:(g + 1) * 128],
                       self.ptt[0:64, :], reads=["ptt"], writes=["qh2"])
            kb.dma("sync", self.xt[:, 0:512],
                   d["k"][g * 128:(g + 1) * 128, :], reads=["qkv2"],
                   writes=["xt"])
            for h in range(8):
                ps, pk = self.psum()
                pv = ps[0:64, 0:128]
                kb.emit("tensor", (lambda pv, h: lambda e: e.transpose(
                    pv, self.xt[:, h * 64:(h + 1) * 64], self.idn))(pv, h),
                        reads=["xt", "idn"], writes=[pk])
                kb.emit("vector", (lambda pv: lambda e: e.tensor_copy(
                    self.ptt[0:64, :], pv))(pv), reads=[pk], writes=["ptt"])
                kb.dma("sync", d["kh"][h][:, g * 128:(g + 1) * 128],
                       self.ptt[0:64, :], reads=["ptt"], writes=["kh2"])
            kb.dma("sync", self.xt[:, 0:512],
                   d["v"][g * 128:(g + 1) * 128, :], reads=["qkv2"],
                   writes=["xt"])
            dd = self.fA[:, g * 512:(g + 1) * 512]
            kb.emit("vector", (lambda dd: lambda e: e.tensor_copy(
                dd, self.xt[:, 0:512]))(dd), reads=["xt"], writes=["fA"])
        for h in range(16):
            kb.dma("sync", self.khbuf[0:64, :], d["kh"][h // 2],
                   reads=["kh2"], writes=["khbuf"])
            for g in range(NT):
                kb.dma("sync", self.qfbuf[0:64, 0:128],
                       d["qh"][h][:, g * 128:(g + 1) * 128],
                       reads=["qh2"], writes=["qfbuf"])
                for n in range(4):
                    pA, pkA = self.psum(6)
                    kb.emit("tensor", (lambda pA, n: lambda e: e.matmul(
                        pA, self.qfbuf[0:64, 0:128],
                        self.khbuf[0:64, n * 512:(n + 1) * 512],
                        start=True, stop=True))(pA, n),
                            reads=["qfbuf", "khbuf"], writes=[pkA])
                    kb.emit("vector", (lambda pA, n: lambda e: e.tensor_copy(
                        self.sS[:, n * 512:(n + 1) * 512], pA))(pA, n),
                            reads=[pkA], writes=["sS"])
                m = self.tiny[:, 2:3]
                nb = self.tiny[:, 3:4]
                dn = self.tiny[:, 4:5]
                rdn = self.tiny[:, 5:6]
                kb.emit("vector", lambda e: e.reduce_max(m, self.sS, AX),
                        reads=["sS"], writes=["tiny"])
                kb.emit("vector", lambda e: e.tensor_scalar(
                    nb, m, -0.125, None, ALU.mult), reads=["tiny"],
                        writes=["tiny"])
                kb.emit("scalar", lambda e: e.activation(
                    self.sP, self.sS, ACT.Exp, bias=nb, scale=0.125,
                    accum_out=dn), reads=["sS", "tiny"], writes=["sP", "tiny"])
                kb.emit("vector", lambda e: e.reciprocal(rdn, dn),
                        reads=["tiny"], writes=["tiny"])
                po, pko = self.pb[7], "pb7"
                for sc in range(NT):
                    ps, pk = self.psum(6)
                    pv = ps[:, 0:128]
                    kb.emit("tensor", (lambda pv, sc: lambda e: e.transpose(
                        pv, self.sP[:, sc * 128:(sc + 1) * 128], self.idn))(pv, sc),
                            reads=["sP", "idn"], writes=[pk])
                    kb.emit("vector", (lambda pv: lambda e: e.tensor_copy(
                        self.ptt, pv))(pv), reads=[pk], writes=["ptt"])
                    rhs = self.fA[:, sc * 512 + (h // 2) * 64:
                                  sc * 512 + (h // 2) * 64 + 64]
                    kb.emit("tensor", (lambda po, rhs, st, sp: lambda e: e.matmul(
                        po[:, 0:64], self.ptt, rhs, start=st, stop=sp))(
                            po, rhs, (sc == 0), (sc == NT - 1)),
                            reads=["ptt", "fA"], writes=[pko])
                kb.emit("vector", (lambda po: lambda e: e.tensor_scalar(
                    self.scr1[:, 0:64], po[:, 0:64], rdn, None, ALU.mult))(po),
                        reads=[pko, "tiny"], writes=["scr1"])
                kb.dma("sync", d["ao"][g * 128:(g + 1) * 128,
                                       h * 64:(h + 1) * 64],
                       self.scr1[:, 0:64], reads=["scr1"], writes=["ao2"])
        if DL2 == 2:
            kb.dma("sync", self.x_out, d["ao"], reads=["ao2"], writes=["x_out"])
            return
        def src_ao(g):
            kb.dma("sync", self.xt, d["ao"][g * 128:(g + 1) * 128, :],
                   reads=["ao2"], writes=["xt"])
        self.emit_ln_stats(src_ao)
        kb.dma("sync", self.bc1, d["lng"].unsqueeze(0).broadcast_to((128, 1024)),
               writes=["bc1"])
        kb.dma("sync", self.sP[:, 0:1024],
               d["lnb"].unsqueeze(0).broadcast_to((128, 1024)), writes=["sP"])
        for g in range(NT):
            src_ao(g)
            kb.emit("vector", (lambda g: lambda e: e.tensor_scalar(
                self.xnt, self.xt, self.statm[:, g:g + 1],
                self.statr[:, g:g + 1], ALU.subtract, ALU.mult))(g),
                    reads=["xt", "statm", "statr"], writes=["xnt"])
            kb.emit("vector", lambda e: e.tensor_mul(
                self.xt, self.xnt, self.bc1), reads=["xnt", "bc1"],
                    writes=["xt"])
            kb.emit("vector", lambda e: e.tensor_add(
                self.xnt, self.xt, self.sP[:, 0:1024]),
                    reads=["xt", "sP"], writes=["xnt"])
            self._rms_actquant_tile(g, self.fmv1[:, g:g + 1])
            self.emit_tm_to_fm(self.xnt, "xnt", self.fB, "fB", g)
        kb.dma("sync", sws, d["sscr"][3:4].unsqueeze(0).broadcast_to((128, 1)),
               reads=["sscr"], writes=["tiny"])
        kb.dma("sync", self.bc1, d["bo"].unsqueeze(0).broadcast_to((128, 1024)),
               writes=["bc1"])
        def evac_o(pA, pkA, g, n):
            kb.emit("vector", (lambda pA, g: lambda e: e.tensor_scalar(
                self.scr1, pA, self.fmv1[:, g:g + 1], self.tiny[:, 14:15],
                ALU.mult, ALU.mult))(pA, g),
                    reads=[pkA, "fmv1", "tiny"], writes=["scr1"])
            kb.emit("vector", (lambda n: lambda e: e.tensor_add(
                self.scr1, self.scr1, self.bc1[:, n * 512:(n + 1) * 512]))(n),
                    reads=["scr1", "bc1"], writes=["scr1"])
            self.emit_resid(self.scr1, "scr1", g, n * 512, 512, 1.0, last)
        self.proj_tm(self.fB, "fB", d["woT"], ND, 1024, evac_o)

    def decl_l0f(self):
        d = {}
        d["w1"] = self.din("l0f_w1", (10, 4, 128, 1024))
        d["w3"] = self.din("l0f_w3", (10, 4, 128, 1024))
        d["w2"] = self.din("l0f_w2", (10, 4, 128, 1024))
        d["rw"] = self.din("l0f_rw", (8, 128, 8))
        d["rb"] = self.din("l0f_rb", (8,))
        d["gates"] = self.dscr("l0f_gates", (8, 2048))
        return d

    def emit_l0f(self, d, scale, last=False):
        kb = self.kb
        self.emit_ln_full(self.fA, "fA")
        kb.dma("sync", self.fmv2[:, 0:8],
               d["rb"].unsqueeze(0).broadcast_to((128, 8)), writes=["fmv2"])
        def evr(pA, pkA, g, n):
            lg = self.scr1[:, 0:8]
            kb.emit("vector", (lambda pA: lambda e: e.tensor_add(
                lg, pA, self.fmv2[:, 0:8]))(pA),
                    reads=[pkA, "fmv2"], writes=["scr1"])
            v1 = self.tiny[:, 20:21]
            v2 = self.tiny[:, 21:22]
            mx = self.tiny[:, 24:32]
            kb.emit("vector", (lambda mx, lg: lambda e: e.max(
                mx, lg))(mx, lg), reads=["scr1"], writes=["tiny"])
            kb.emit("vector", lambda e: e.tensor_copy(v1, self.tiny[:, 24:25]),
                    reads=["tiny"], writes=["tiny"])
            kb.emit("vector", lambda e: e.tensor_copy(v2, self.tiny[:, 25:26]),
                    reads=["tiny"], writes=["tiny"])
            d12 = self.tiny[:, 22:23]
            kb.emit("vector", lambda e: e.tensor_sub(d12, v2, v1),
                    reads=["tiny"], writes=["tiny"])
            g2 = self.tiny[:, 23:24]
            kb.emit("scalar", lambda e: e.activation(g2, d12, ACT.Sigmoid),
                    reads=["tiny"], writes=["tiny"])
            g1 = self.tiny[:, 22:23]
            kb.emit("vector", lambda e: e.tensor_scalar(
                g1, g2, -1.0, 1.0, ALU.mult, ALU.add),
                    reads=["tiny"], writes=["tiny"])
            t1 = self.scr1[:, 8:16]
            t2 = self.scr1[:, 16:24]
            kb.emit("vector", (lambda lg: lambda e: e.tensor_scalar(
                t1, lg, v1, g1, ALU.is_equal, ALU.mult))(lg),
                    reads=["scr1", "tiny"], writes=["scr1"])
            kb.emit("vector", (lambda lg: lambda e: e.tensor_scalar(
                t2, lg, v2, g2, ALU.is_equal, ALU.mult))(lg),
                    reads=["scr1", "tiny"], writes=["scr1"])
            gt = self.scr2[:, 0:8]
            kb.emit("vector", lambda e: e.tensor_add(gt, t1, t2),
                    reads=["scr1"], writes=["scr2"])
            ps, pk = self.psum()
            kb.emit("tensor", (lambda ps: lambda e: e.transpose(
                ps[0:8, 0:128], gt, self.idn))(ps),
                    reads=["scr2", "idn"], writes=[pk])
            kb.emit("vector", (lambda ps, g: lambda e: e.tensor_copy(
                self.khbuf[0:8, g * 128:(g + 1) * 128], ps[0:8, 0:128]))(ps, g),
                    reads=[pk], writes=["khbuf"])
        self.proj_tm(self.fA, "fA", d["rw"], ND, 8, evr)
        kb.dma("sync", d["gates"], self.khbuf[0:8, :], reads=["khbuf"],
               writes=["gatesd"])
        for si in range(4):
            acc = [self.sS[:, 0:1024], self.sS[:, 1024:2048],
                   self.sP[:, 0:1024], self.sP[:, 1024:2048]]
            acck = ["sS", "sS", "sP", "sP"]
            for gi in range(4):
                kb.emit("vector", (lambda gi: lambda e: e.memset(
                    acc[gi], 0.0))(gi), writes=[acck[gi]])
            for half in range(2):
                for ei in range(5):
                    e_ = half * 5 + ei
                    if e_ < 8:
                        kb.dma("sync", self.csbuf[:, 0:512],
                               d["gates"][e_, si * 512:(si + 1) * 512]
                               .unsqueeze(0).broadcast_to((128, 512)),
                               reads=["gatesd"], writes=["csbuf"])
                    for m_ in range(4):
                        bwA = self.wbufF[0][:, 0:1024]
                        bwB = self.wbufF[1][:, 0:1024]
                        kb.dma("scalar", bwA, d["w1"][e_][m_], writes=["wb0"])
                        kb.dma("gpsimd", bwB, d["w3"][e_][m_], writes=["wb1"])
                        pA, pkA = self.psum()
                        pB, pkB = self.psum()
                        for (pp, pk_, bw_, kw_) in ((pA, pkA, bwA, "wb0"),
                                                    (pB, pkB, bwB, "wb1")):
                            for k in range(ND):
                                kb.emit("tensor", (lambda pp, lt, rhs, st, sp:
                                        lambda e: e.matmul(pp, lt, rhs,
                                                           start=st, stop=sp))(
                                            pp, bw_[:, k * 128:(k + 1) * 128],
                                            self.fA[:, k * T + si * 512:
                                                    k * T + (si + 1) * 512],
                                            (k == 0), (k == ND - 1)),
                                        reads=[kw_, "fA"], writes=[pk_])
                        kb.emit("scalar", (lambda pA: lambda e: e.activation(
                            self.scr1, pA, ACT.Silu))(pA),
                                reads=[pkA], writes=["scr1"])
                        kb.emit("vector", (lambda pB: lambda e: e.tensor_mul(
                            self.scr2, pB, self.scr1))(pB),
                                reads=[pkB, "scr1"], writes=["scr2"])
                        hc = self.fB[:, (ei * 4 + m_) * 512:
                                     (ei * 4 + m_ + 1) * 512]
                        if e_ < 8:
                            kb.emit("vector", (lambda hc: lambda e: e.tensor_mul(
                                hc, self.scr2, self.csbuf[:, 0:512]))(hc),
                                    reads=["scr2", "csbuf"], writes=["fB"])
                        else:
                            kb.emit("vector", (lambda hc: lambda e: e.tensor_copy(
                                hc, self.scr2))(hc), reads=["scr2"],
                                    writes=["fB"])
                for n in range(2):
                    pbs = [self.psum() for _ in range(4)]
                    nk = 0
                    for ei in range(5):
                        e_ = half * 5 + ei
                        for k in range(4):
                            kw = "wb%d" % (nk % 2)
                            bw = self.wbufF[nk % 2][:, 0:512]
                            kb.dma("scalar", bw,
                                   d["w2"][e_][k][:, n * 512:(n + 1) * 512],
                                   writes=[kw])
                            for gi in range(4):
                                pA, pkA = pbs[gi]
                                lt = self.fB[:, (ei * 4 + k) * 512 + gi * 128:
                                             (ei * 4 + k) * 512 + (gi + 1) * 128]
                                kb.emit("tensor", (lambda pA, lt, bw, st, sp:
                                        lambda e: e.matmul(pA, lt, bw,
                                                           start=st, stop=sp))(
                                            pA, lt, bw, (nk == 0), (nk == 19)),
                                        reads=[kw, "fB"], writes=[pkA])
                            nk += 1
                    for gi in range(4):
                        pA, pkA = pbs[gi]
                        dsl = acc[gi][:, n * 512:(n + 1) * 512]
                        kb.emit("vector", (lambda dsl, pA: lambda e: e.tensor_add(
                            self.scr1, pA, dsl))(dsl, pA),
                                reads=[pkA, acck[gi]], writes=["scr1"])
                        kb.emit("vector", (lambda dsl: lambda e: e.tensor_copy(
                            dsl, self.scr1))(dsl), reads=["scr1"],
                                writes=[acck[gi]])
            for gi in range(4):
                g = si * 4 + gi
                for n in range(2):
                    self.emit_resid(acc[gi][:, n * 512:(n + 1) * 512],
                                    acck[gi], g, n * 512, 512, scale, last)

    CLOC = 1280

    def decl_l2f(self):
        d = {}
        d["wrT"] = self.din("l2f_wrT", (8, 128, 8))
        d["w12"] = self.din("l2f_w12", (8, 64, 128, 1024))
        d["w3"] = self.din("l2f_w3", (8, 32, 128, 1024))
        d["lng"] = self.din("l2f_lng", (1024,))
        d["lnb"] = self.din("l2f_lnb", (1024,))
        d["iota8"] = self.din("l2f_iota8", (128, 8))
        d["cmask"] = self.din("l2f_cmask", (8,))
        d["xnscr"] = self.dscr("l2f_xnscr", (2048, 1024))
        d["bufx"] = self.dscr("l2f_bufx", (8 * self.CLOC, 1024))
        d["bufy"] = self.dscr("l2f_bufy", (8 * self.CLOC, 1024))
        d["zrow"] = self.dscr("l2f_zrow", (1, 1024))
        d["ccin"] = self.dscr("l2f_ccin", (1, 8))
        d["ccout"] = self.dscr("l2f_ccout", (1, 64))
        return d

    def _l2f_keptpos_tile(self, d, g, k, iota_bc):
        kb = self.kb
        oh = self.scr2[:, 0:8]
        ek = self.fmv1[:, 32 + 16 * k + g:32 + 16 * k + g + 1]
        kb.emit("vector", (lambda oh, ek: lambda e: e.tensor_scalar(
            oh, iota_bc, ek, None, ALU.is_equal))(oh, ek),
                reads=["bc2i", "fmv1"], writes=["scr2"])
        t = self.scr2[:, 8:16]
        kb.emit("vector", (lambda t, oh: lambda e: e.tensor_mul(
            t, oh, self.scr1[:, 0:8]))(t, oh), reads=["scr2", "scr1"],
                writes=["scr2"])
        keptk = self.tiny[:, 40 + k:41 + k]
        kb.emit("vector", (lambda keptk, t: lambda e: e.reduce_sum(
            keptk, t, AX))(keptk, t), reads=["scr2"], writes=["tiny"])
        kb.emit("vector", (lambda t, oh: lambda e: e.tensor_mul(
            t, oh, self.scr1[:, 8:16]))(t, oh), reads=["scr2", "scr1"],
                writes=["scr2"])
        posk = self.tiny[:, 42 + k:43 + k]
        kb.emit("vector", (lambda posk, t: lambda e: e.reduce_sum(
            posk, t, AX))(posk, t), reads=["scr2"], writes=["tiny"])
        gcol = self.fmv1[:, 16 * k + g:16 * k + g + 1]
        gt = self.tiny[:, 44 + k:45 + k]
        kb.emit("vector", (lambda gt, gcol, keptk: lambda e: e.tensor_tensor(
            gt, gcol, keptk, ALU.mult))(gt, gcol, keptk),
                reads=["fmv1", "tiny"], writes=["tiny"])
        kb.emit("vector", (lambda gcol, gt: lambda e: e.tensor_copy(
            gcol, gt))(gcol, gt), reads=["tiny"], writes=["fmv1"])
        idxf = self.tiny[:, 46 + k:47 + k]
        kb.emit("vector", (lambda idxf, ek, posk: lambda e:
                e.scalar_tensor_tensor(idxf, ek, float(self.CLOC), posk,
                                       ALU.mult, ALU.add))(idxf, ek, posk),
                reads=["fmv1", "tiny"], writes=["tiny"])
        ix = self.idx32[:, k:k + 1]
        kb.emit("vector", (lambda ix, idxf: lambda e: e.tensor_copy(
            ix, idxf))(ix, idxf), reads=["tiny"], writes=["idx32"])
        return ix

    def _l2f_transpose_rank_tile(self, g):
        kb = self.kb
        for src, dst_c in ((self.sS, 0), (self.sP, 8)):
            ps, pk = self.psum()
            kb.emit("tensor", (lambda ps, src, g: lambda e: e.transpose(
                ps[0:128, 0:8], src[0:8, g * 128:(g + 1) * 128],
                self.idn[0:8, 0:8]))(ps, src, g),
                    reads=["sS", "sP", "idn"], writes=[pk])
            kb.emit("vector", (lambda ps, dst_c: lambda e: e.tensor_copy(
                self.scr1[:, dst_c:dst_c + 8], ps[0:128, 0:8]))(ps, dst_c),
                    reads=[pk], writes=["scr1"])

    def emit_l2f(self, d, scale, last=False):
        kb = self.kb
        iota_bc = self.bc1[:, 0:8]
        self.emit_ln_stats()
        kb.dma("sync", self.csbuf[:, 0:1024] if False else self.bc1,
               d["lng"].unsqueeze(0).broadcast_to((128, 1024)), writes=["bc1"])
        kb.dma("sync", self.sP[:, 0:1024],
               d["lnb"].unsqueeze(0).broadcast_to((128, 1024)), writes=["sP"])
        for g in range(NT):
            kb.dma("sync", self.xt, self.x_dram[g * 128:(g + 1) * 128, :],
                   reads=["x_dram"], writes=["xt"])
            kb.emit("vector", (lambda g: lambda e: e.tensor_scalar(
                self.xnt, self.xt, self.statm[:, g:g + 1],
                self.statr[:, g:g + 1], ALU.subtract, ALU.mult))(g),
                    reads=["xt", "statm", "statr"], writes=["xnt"])
            kb.emit("vector", lambda e: e.tensor_mul(
                self.xt, self.xnt, self.bc1), reads=["xnt", "bc1"],
                    writes=["xt"])
            kb.emit("vector", lambda e: e.tensor_add(
                self.xnt, self.xt, self.sP[:, 0:1024]),
                    reads=["xt", "sP"], writes=["xnt"])
            kb.dma("sync", d["xnscr"][g * 128:(g + 1) * 128, :], self.xnt,
                   reads=["xnt"], writes=["xnscr"])
            self.emit_tm_to_fm(self.xnt, "xnt", self.fA, "fA", g)
        iob = self.csbuf[:, 512:520]
        kb.dma("sync", iob, d["iota8"][:, 0:8], writes=["bc2i"])
        def evr(pA, pkA, g, n):
            lg = self.scr1[:, 0:8]
            kb.emit("vector", (lambda lg, pA: lambda e: e.tensor_scalar(
                lg, pA, -10000.0, 10000.0, ALU.max, ALU.min))(lg, pA),
                    reads=[pkA], writes=["scr1"])
            mx = self.tiny[:, 24:32]
            kb.emit("vector", (lambda mx, lg: lambda e: e.max(
                mx, lg))(mx, lg), reads=["scr1"], writes=["tiny"])
            nv1 = self.tiny[:, 32:33]
            kb.emit("vector", lambda e: e.tensor_scalar(
                nv1, self.tiny[:, 24:25], -1.0, None, ALU.mult),
                    reads=["tiny"], writes=["tiny"])
            ex = self.scr1[:, 8:16]
            sm = self.tiny[:, 33:34]
            kb.emit("scalar", (lambda ex, lg: lambda e: e.activation(
                ex, lg, ACT.Exp, bias=nv1, accum_out=sm))(ex, lg),
                    reads=["scr1", "tiny"], writes=["scr1", "tiny"])
            kb.emit("vector", lambda e: e.tensor_scalar(
                self.tiny[:, 34:35], sm, 1e-12, None, ALU.add),
                    reads=["tiny"], writes=["tiny"])
            kb.emit("vector", lambda e: e.reciprocal(
                self.tiny[:, 35:36], self.tiny[:, 34:35]),
                    reads=["tiny"], writes=["tiny"])
            pr = self.scr2[:, 0:8]
            kb.emit("vector", (lambda pr, ex: lambda e: e.tensor_scalar(
                pr, ex, self.tiny[:, 35:36], None, ALU.mult))(pr, ex),
                    reads=["scr1", "tiny"], writes=["scr2"])
            pmx = self.tiny[:, 24:32]
            kb.emit("vector", (lambda pmx, pr: lambda e: e.max(
                pmx, pr))(pmx, pr), reads=["scr2"], writes=["tiny"])
            kb.emit("vector", (lambda g: lambda e: e.tensor_copy(
                self.fmv1[:, g:g + 1], self.tiny[:, 24:25]))(g),
                    reads=["tiny"], writes=["fmv1"])
            kb.emit("vector", (lambda g: lambda e: e.tensor_copy(
                self.fmv1[:, 16 + g:17 + g], self.tiny[:, 25:26]))(g),
                    reads=["tiny"], writes=["fmv1"])
            oh1 = self.scr1[:, 16:24]
            oh2 = self.scr1[:, 24:32]
            kb.emit("vector", (lambda oh1, pr: lambda e: e.tensor_scalar(
                oh1, pr, self.tiny[:, 24:25], None, ALU.is_equal))(oh1, pr),
                    reads=["scr2", "tiny"], writes=["scr1"])
            kb.emit("vector", (lambda oh2, pr: lambda e: e.tensor_scalar(
                oh2, pr, self.tiny[:, 25:26], None, ALU.is_equal))(oh2, pr),
                    reads=["scr2", "tiny"], writes=["scr1"])
            t8 = self.scr2[:, 8:16]
            kb.emit("vector", (lambda t8, oh1: lambda e: e.tensor_mul(
                t8, oh1, iob))(t8, oh1), reads=["scr1", "bc2i"],
                    writes=["scr2"])
            kb.emit("vector", (lambda g: lambda e: e.reduce_sum(
                self.fmv1[:, 32 + g:33 + g], self.scr2[:, 8:16], AX))(g),
                    reads=["scr2"], writes=["fmv1"])
            kb.emit("vector", (lambda t8, oh2: lambda e: e.tensor_mul(
                t8, oh2, iob))(t8, oh2), reads=["scr1", "bc2i"],
                    writes=["scr2"])
            kb.emit("vector", (lambda g: lambda e: e.reduce_sum(
                self.fmv1[:, 48 + g:49 + g], self.scr2[:, 8:16], AX))(g),
                    reads=["scr2"], writes=["fmv1"])
            it = self.scr2[:, 16:24]
            kb.emit("vector", (lambda it, oh1, oh2: lambda e: e.tensor_add(
                it, oh1, oh2))(it, oh1, oh2), reads=["scr1"], writes=["scr2"])
            ps2, pk2 = self.psum()
            kb.emit("tensor", (lambda ps2, it: lambda e: e.transpose(
                ps2[0:8, 0:128], it, self.idn))(ps2, it),
                    reads=["scr2", "idn"], writes=[pk2])
            kb.emit("vector", (lambda ps2, g: lambda e: e.tensor_copy(
                self.khbuf[0:8, g * 128:(g + 1) * 128], ps2[0:8, 0:128]))(
                    ps2, g), reads=[pk2], writes=["khbuf"])
        self.proj_tm(self.fA, "fA", d["wrT"], ND, 8, evr)
        kb.emit("vector", lambda e: e.memset(self.sS[0:8, :], 1.0),
                writes=["sS"])
        kb.emit("vector", lambda e: e.tensor_tensor_scan(
            self.sP[0:8, :], self.sS[0:8, :], self.khbuf[0:8, :], 0.0,
            ALU.mult, ALU.add), reads=["sS", "khbuf"], writes=["sP"])
        kb.dma("gpsimd", d["ccin"], self.sP[0:8, 2047:2048], reads=["sP"],
               writes=["ccin"])
        kb.emit("gpsimd", lambda e: e.collective_compute(
            "AllGather", ALU.bypass, replica_groups=[list(range(8))],
            ins=[d["ccin"]], outs=[d["ccout"]]),
                reads=["ccin"], writes=["ccout"])
        cc = self.scr2[0:8, 0:8]
        kb.dma("sync", cc, d["ccout"].rearrange("a (b c) -> (a b) c", b=8),
               reads=["ccout"], writes=["scr2"])
        cmk = self.tiny[0:8, 48:49]
        kb.dma("sync", cmk, d["cmask"].unsqueeze(1), writes=["tiny"])
        msk = self.scr2[0:8, 8:16]
        kb.emit("vector", lambda e: e.tensor_scalar(
            msk, cc, cmk, None, ALU.mult), reads=["scr2", "tiny"],
                writes=["scr2"])
        one8 = self.tiny[0:8, 49:50]
        kb.emit("vector", lambda e: e.memset(one8, 1.0), writes=["tiny"])
        ps3, pk3 = self.psum()
        kb.emit("tensor", lambda e: e.matmul(ps3[0:1, 0:8], one8, msk,
                                             start=True, stop=True),
                reads=["scr2", "tiny"], writes=[pk3])
        baser = self.scr2[0:1, 16:24]
        kb.emit("vector", lambda e: e.tensor_copy(baser, ps3[0:1, 0:8]),
                reads=[pk3], writes=["scr2"])
        ps4, pk4 = self.psum()
        kb.emit("tensor", lambda e: e.transpose(
            ps4[0:8, 0:1], baser, self.idn[0:1, 0:1]),
                reads=["scr2", "idn"], writes=[pk4])
        basec = self.tiny[0:8, 50:51]
        kb.emit("vector", lambda e: e.tensor_copy(basec, ps4[0:8, 0:1]),
                reads=[pk4], writes=["tiny"])
        kb.emit("vector", lambda e: e.tensor_scalar(
            self.sS[0:8, :], self.sP[0:8, :], basec, 5121.0, ALU.add,
            ALU.is_lt), reads=["sP", "tiny"], writes=["sS"])
        kb.emit("vector", lambda e: e.tensor_scalar(
            self.sP[0:8, :], self.sP[0:8, :], -1.0, None, ALU.add),
                reads=["sP"], writes=["sP"])
        kb.emit("vector", lambda e: e.memset(self.scr1[0:1, 0:512], 0.0),
                writes=["scr1"])
        kb.dma("sync", d["zrow"][0:1, 0:512], self.scr1[0:1, 0:512],
               reads=["scr1"], writes=["zrow"])
        kb.dma("sync", d["zrow"][0:1, 512:1024], self.scr1[0:1, 0:512],
               reads=["scr1"], writes=["zrow"])
        for i in range(8):
            kb.dma("gpsimd", d["bufx"][i * self.CLOC:(i + 1) * self.CLOC, :],
                   d["zrow"].broadcast_to((self.CLOC, 1024)),
                   reads=["zrow"], writes=["bufx"])
        for g in range(NT):
            self._l2f_transpose_rank_tile(g)
            kb.dma("sync", self.xt, d["xnscr"][g * 128:(g + 1) * 128, :],
                   reads=["xnscr"], writes=["xt"])
            for k in range(2):
                ix = self._l2f_keptpos_tile(d, g, k, iob)
                kb.emit("gpsimd", (lambda ix: lambda e: e.indirect_dma_start(
                    out=d["bufx"], out_offset=IndirectOffsetOnAxis(ix, 0),
                    in_=self.xt, in_offset=None))(ix),
                        reads=["idx32", "xt", "bufx"], writes=["bufx"],
                        dma=True)
        for e_ in range(8):
            for s0, w in ((0, 512), (512, 512), (1024, 256)):
                ntile = w // 128
                for ti in range(ntile):
                    r0 = e_ * self.CLOC + s0 + ti * 128
                    kb.dma("sync", self.xt, d["bufx"][r0:r0 + 128, :],
                           reads=["bufx"], writes=["xt"])
                    for c in range(ND):
                        ps, pk = self.psum()
                        pv = ps[:, 0:128]
                        kb.emit("tensor", (lambda pv, c: lambda e: e.transpose(
                            pv, self.xt[:, c * 128:(c + 1) * 128], self.idn))(
                                pv, c), reads=["xt", "idn"], writes=[pk])
                        dd = self.fB[:, c * w + ti * 128: c * w + (ti + 1) * 128]
                        kb.emit("vector", (lambda dd, pv: lambda e: e.tensor_copy(
                            dd, pv))(dd, pv), reads=[pk], writes=["fB"])
                for m_ in range(32):
                    bwA = self.wbufF[0][:, 0:1024]
                    bwB = self.wbufF[1][:, 0:1024]
                    kb.dma("scalar", bwA, d["w12"][e_][m_], writes=["wb0"])
                    kb.dma("gpsimd", bwB, d["w12"][e_][32 + m_], writes=["wb1"])
                    pA, pkA = self.psum()
                    pB, pkB = self.psum()
                    for (pp, pk_, bw_, kw_) in ((pA, pkA, bwA, "wb0"),
                                                (pB, pkB, bwB, "wb1")):
                        po = pp[:, 0:w] if w < 512 else pp
                        for k in range(ND):
                            kb.emit("tensor", (lambda po, lt, rhs, st, sp:
                                    lambda e: e.matmul(po, lt, rhs, start=st,
                                                       stop=sp))(
                                        po, bw_[:, k * 128:(k + 1) * 128],
                                        self.fB[:, k * w:(k + 1) * w],
                                        (k == 0), (k == ND - 1)),
                                    reads=[kw_, "fB"], writes=[pk_])
                    kb.emit("scalar", (lambda pA, w: lambda e: e.activation(
                        self.scr1[:, 0:w], pA[:, 0:w], ACT.Silu))(pA, w),
                            reads=[pkA], writes=["scr1"])
                    hd = self.fA[:, m_ * w:(m_ + 1) * w]
                    kb.emit("vector", (lambda hd, pB, w: lambda e: e.tensor_mul(
                        hd, pB[:, 0:w], self.scr1[:, 0:w]))(hd, pB, w),
                            reads=[pkB, "scr1"], writes=["fA"])
                for n in range(2):
                    pbs = [self.psum() for _ in range(ntile)]
                    for k in range(32):
                        kw = "wb%d" % (k % 2)
                        bw = self.wbufF[k % 2][:, 0:512]
                        kb.dma("scalar", bw,
                               d["w3"][e_][k][:, n * 512:(n + 1) * 512],
                               writes=[kw])
                        for ti in range(ntile):
                            pA, pkA = pbs[ti]
                            lt = self.fA[:, k * w + ti * 128:
                                         k * w + (ti + 1) * 128]
                            kb.emit("tensor", (lambda pA, lt, bw, st, sp:
                                    lambda e: e.matmul(pA, lt, bw, start=st,
                                                       stop=sp))(
                                        pA, lt, bw, (k == 0), (k == 31)),
                                    reads=[kw, "fA"], writes=[pkA])
                    for ti in range(ntile):
                        pA, pkA = pbs[ti]
                        r0 = e_ * self.CLOC + s0 + ti * 128
                        kb.emit("vector", (lambda pA: lambda e: e.tensor_copy(
                            self.scr1, pA))(pA), reads=[pkA], writes=["scr1"])
                        kb.dma("sync", d["bufy"][r0:r0 + 128,
                                                 n * 512:(n + 1) * 512],
                               self.scr1, reads=["scr1"], writes=["bufy"])
        for g in range(NT):
            self._l2f_transpose_rank_tile(g)
            for k in range(2):
                ix = self._l2f_keptpos_tile(d, g, k, iob)
                kb.emit("gpsimd", (lambda ix: lambda e: e.indirect_dma_start(
                    out=self.xt, out_offset=None, in_=d["bufy"],
                    in_offset=IndirectOffsetOnAxis(ix, 0)))(ix),
                        reads=["idx32", "bufy"], writes=["xt"], dma=True)
                if k == 0:
                    kb.emit("vector", (lambda g: lambda e: e.tensor_scalar(
                        self.xnt, self.xt, self.fmv1[:, g:g + 1], None,
                        ALU.mult))(g), reads=["xt", "fmv1"], writes=["xnt"])
                else:
                    kb.emit("vector", (lambda g: lambda e:
                            e.scalar_tensor_tensor(
                                self.fB[:, 0:1024], self.xt,
                                self.fmv1[:, 16 + g:17 + g], self.xnt,
                                ALU.mult, ALU.add))(g),
                            reads=["xt", "xnt", "fmv1"], writes=["fB"])
            for n in range(2):
                self.emit_resid(self.fB[:, n * 512:(n + 1) * 512], "fB",
                                g, n * 512, 512, scale, last)

    def decl_l3a(self):
        d = {}
        d["win2"] = self.din("l3_win2", (8, 128, 1024))
        d["wgate"] = self.din("l3_wgate", (8, 128, 1024))
        d["bin2"] = self.din("l3_bin2", (128, 8))
        d["bgate"] = self.din("l3_bgate", (128, 8))
        d["A"] = self.din("l3_A", (16, 16))
        d["bmT"] = self.din("l3_bmT", (128, 128))
        d["cmT"] = self.din("l3_cmT", (16, 1024))
        d["woutT"] = self.din("l3_woutT", (8, 128, 1024))
        d["bout"] = self.din("l3_bout", (1024,))
        return d

    def emit_l3a(self, d, last=False):
        CUT = int(os.environ.get("L3CUT", "6"))
        kb = self.kb
        self.emit_ln_full(self.fA, "fA")
        kb.dma("sync", self.fmv1[:, 0:8], d["bin2"], writes=["fmv1"])
        kb.dma("sync", self.fmv2[:, 0:8], d["bgate"], writes=["fmv2"])
        kb.dma("sync", self.bc1, d["bout"].unsqueeze(0).broadcast_to((128, 1024)),
               writes=["bc1"])
        bmT = self.qfbuf[:, 0:128]
        kb.dma("sync", bmT, d["bmT"], writes=["qfbuf"])
        cmT = self.csbuf[0:16, 0:1024]
        kb.dma("sync", cmT, d["cmT"], writes=["csbuf"])
        for m in range(ND):
            kwA, kwB = "w20", "w21"
            bwA = self.wbufF[0][:, 0:1024]
            bwB = self.wbufF[1][:, 0:1024]
            kb.dma("scalar", bwA, d["win2"][m], writes=["wb0"])
            kb.dma("gpsimd", bwB, d["wgate"][m], writes=["wb1"])
            for si in range(4):
                pA, pkA = self.psum()
                pB, pkB = self.psum()
                for (pp, pk_, bw_, kw_) in ((pA, pkA, bwA, "wb0"),
                                            (pB, pkB, bwB, "wb1")):
                    for k in range(ND):
                        lt = bw_[:, k * 128:(k + 1) * 128]
                        rhs = self.fA[:, k * T + si * 512:(k * T + (si + 1) * 512)]
                        kb.emit("tensor", (lambda pp, lt, rhs, st, sp:
                                lambda e: e.matmul(pp, lt, rhs, start=st, stop=sp))(
                                    pp, lt, rhs, (k == 0), (k == ND - 1)),
                                reads=[kw_, "fA"], writes=[pk_])
                kb.emit("scalar", (lambda pB, m: lambda e: e.activation(
                    self.scr1, pB, ACT.Sigmoid, bias=self.fmv2[:, m:m + 1]))(pB, m),
                        reads=[pkB, "fmv2"], writes=["scr1"])
                kb.emit("vector", (lambda pA, m: lambda e: e.tensor_scalar(
                    self.scr2, pA, self.fmv1[:, m:m + 1], None, ALU.add))(pA, m),
                        reads=[pkA, "fmv1"], writes=["scr2"])
                dd = self.fB[:, m * T + si * 512: m * T + (si + 1) * 512]
                kb.emit("vector", (lambda dd: lambda e: e.tensor_mul(
                    dd, self.scr2, self.scr1))(dd),
                        reads=["scr1", "scr2"], writes=["fB"])
        if CUT < 2:
            return
        uB = self.khbuf[0:16, :]
        for si in range(4):
            pA, pkA = self.psum()
            pu = pA[0:16, :]
            for k in range(ND):
                lt = bmT[:, k * 16:(k + 1) * 16]
                rhs = self.fB[:, k * T + si * 512: k * T + (si + 1) * 512]
                kb.emit("tensor", (lambda pu, lt, rhs, st, sp:
                        lambda e: e.matmul(pu, lt, rhs, start=st, stop=sp))(
                            pu, lt, rhs, (k == 0), (k == ND - 1)),
                        reads=["qfbuf", "fB"], writes=[pkA])
            kb.emit("vector", (lambda pu, si: lambda e: e.tensor_copy(
                uB[:, si * 512:(si + 1) * 512], pu))(pu, si),
                    reads=[pkA], writes=["khbuf"])
        if CUT < 3:
            return
        A_sb = self.pows[:, 352 - 16:352]
        kb.dma("sync", A_sb, d["A"], writes=["pows"])
        P0 = self.pows[:, 0:16]
        P0T = self.pows[:, 16:32]
        ps, pk = self.psum()
        kb.emit("tensor", lambda e: e.transpose(ps[0:16, 0:16], A_sb,
                                                self.idn[0:16, 0:16]),
                reads=["pows", "idn"], writes=[pk])
        kb.emit("vector", lambda e: e.tensor_copy(P0, ps[0:16, 0:16]),
                reads=[pk], writes=["pows"])
        kb.emit("vector", lambda e: e.tensor_copy(P0T, A_sb),
                reads=["pows"], writes=["pows"])
        for k in range(1, 11):
            Pp = self.pows[:, 32 * (k - 1):32 * (k - 1) + 16]
            PpT = self.pows[:, 32 * (k - 1) + 16:32 * k]
            Pc = self.pows[:, 32 * k:32 * k + 16]
            PcT = self.pows[:, 32 * k + 16:32 * (k + 1)]
            ps, pk = self.psum()
            kb.emit("tensor", (lambda Pp, PpT, ps: lambda e: e.matmul(
                ps[0:16, 0:16], PpT, Pp, start=True, stop=True))(Pp, PpT, ps),
                    reads=["pows"], writes=[pk])
            kb.emit("vector", (lambda Pc, ps: lambda e: e.tensor_copy(
                Pc, ps[0:16, 0:16]))(Pc, ps), reads=[pk], writes=["pows"])
            ps2, pk2 = self.psum()
            kb.emit("tensor", (lambda Pp, PpT, ps2: lambda e: e.matmul(
                ps2[0:16, 0:16], Pp, PpT, start=True, stop=True))(Pp, PpT, ps2),
                    reads=["pows"], writes=[pk2])
            kb.emit("vector", (lambda PcT, ps2: lambda e: e.tensor_copy(
                PcT, ps2[0:16, 0:16]))(PcT, ps2), reads=[pk2], writes=["pows"])
        if CUT < 4:
            return
        hA, hAk = uB, "khbuf"
        hB, hBk = self.sS[0:16, :], "sS"
        g_, gk = self.sP[0:16, :], "sP"
        for k in range(11):
            Pk = self.pows[:, 32 * k:32 * k + 16]
            sh = 1 << k
            for nb in range(4):
                ps, pk = self.psum()
                pu = ps[0:16, :]
                kb.emit("tensor", (lambda pu, Pk, hA, nb: lambda e: e.matmul(
                    pu, Pk, hA[:, nb * 512:(nb + 1) * 512], start=True,
                    stop=True))(pu, Pk, hA, nb),
                        reads=["pows", hAk], writes=[pk])
                kb.emit("vector", (lambda pu, nb: lambda e: e.tensor_copy(
                    g_[:, nb * 512:(nb + 1) * 512], pu))(pu, nb),
                        reads=[pk], writes=[gk])
            kb.emit("vector", (lambda hB, hA, sh: lambda e: e.tensor_copy(
                hB[:, 0:sh], hA[:, 0:sh]))(hB, hA, sh),
                    reads=[hAk], writes=[hBk])
            kb.emit("vector", (lambda hB, hA, sh: lambda e: e.tensor_add(
                hB[:, sh:T], hA[:, sh:T], g_[:, 0:T - sh]))(hB, hA, sh),
                    reads=[hAk, gk], writes=[hBk])
            hA, hAk, hB, hBk = hB, hBk, hA, hAk
        hs, hsk = hA, hAk
        if CUT < 5:
            return
        for m in range(ND):
            for si in range(4):
                pA, pkA = self.psum()
                lt = cmT[:, m * 128:(m + 1) * 128]
                rhs = hs[:, si * 512:(si + 1) * 512]
                kb.emit("tensor", (lambda pA, lt, rhs: lambda e: e.matmul(
                    pA, lt, rhs, start=True, stop=True))(pA, lt, rhs),
                        reads=["csbuf", hsk], writes=[pkA])
                dd = self.fA[:, m * T + si * 512: m * T + (si + 1) * 512]
                kb.emit("vector", (lambda dd, pA: lambda e: e.tensor_copy(
                    dd, pA))(dd, pA), reads=[pkA], writes=["fA"])
        if CUT < 6:
            return
        def evac(pA, pkA, g, n):
            kb.emit("vector", (lambda pA, n: lambda e: e.tensor_add(
                self.scr1, pA, self.bc1[:, n * 512:(n + 1) * 512]))(pA, n),
                    reads=[pkA, "bc1"], writes=["scr1"])
            self.emit_resid(self.scr1, "scr1", g, n * 512, 512, 1.0, last)
        self.proj_tm(self.fA, "fA", d["woutT"], ND, 1024, evac)

    def build(self):
        kb = self.kb
        tern = {}
        for L, st in (("1", ("l1f1", "l1f2")), ("3", ("l3f1", "l3f2"))):
            if any(s in self.steps for s in st):
                tern[L] = self.decl_tffn(L)
                self.stage_tffn(tern[L])
        decls = {}
        if "l3a" in self.steps:
            decls["l3a"] = self.decl_l3a()
        if "l1a" in self.steps:
            decls["l1a"] = self.decl_l1a()
        if "l0a" in self.steps:
            decls["l0a"] = self.decl_l0a()
        if "l2a" in self.steps:
            decls["l2a"] = self.decl_l2a()
        if "l0f1" in self.steps or "l0f2" in self.steps:
            decls["l0f"] = self.decl_l0f()
        if "l2f1" in self.steps or "l2f2" in self.steps:
            decls["l2f"] = self.decl_l2f()
        laststep = self.steps[-1]
        for st in self.steps:
            last = (st == laststep) and DEBUG_A < 0
            if st in ("l1f1", "l1f2"):
                self.emit_tffn(tern["1"], 0.5, last)
            elif st in ("l3f1", "l3f2"):
                self.emit_tffn(tern["3"], 0.5, last)
            elif st == "l3a":
                self.emit_l3a(decls["l3a"], last)
            elif st == "l1a":
                self.emit_l1a(decls["l1a"], last)
            elif st == "l0a":
                self.emit_l0a(decls["l0a"], last)
            elif st == "l2a":
                self.emit_l2a(decls["l2a"], last)
            elif st in ("l0f1", "l0f2"):
                self.emit_l0f(decls["l0f"], 0.5, last)
            elif st in ("l2f1", "l2f2"):
                self.emit_l2f(decls["l2f"], 0.5, last)
            else:
                raise NotImplementedError(st)
            kb.barrier()
        final = [(q, kb.count[q]) for q in DQ.values() if kb.count[q] > 0]
        kb.finalize(final)
        return self.nc



def host_maps(inputs, steps, x_override=None):
    x = x_override if x_override is not None else inputs["L0_attn_x"]
    idn = np.eye(128, dtype=np.float32)
    base = {"idn_f": idn}
    for L in ("1", "3"):
        if not any(s in steps for s in (f"l{L}f1", f"l{L}f2")):
            continue
        p = f"L{L}_ffn_"
        base[f"t{L}_w1"] = tile_lhsT(np.ascontiguousarray(inputs[p + "w1_w"].T), 8, 32)
        base[f"t{L}_w2"] = tile_lhsT(np.ascontiguousarray(inputs[p + "w2_w"].T), 8, 32)
        base[f"t{L}_h1"] = tile_lhsT(np.ascontiguousarray(inputs[p + "h1_w"].T), 32, 32)
        base[f"t{L}_h2"] = tile_lhsT(np.ascontiguousarray(inputs[p + "h2_w"].T), 32, 32)
        base[f"t{L}_w3"] = np.ascontiguousarray(inputs[p + "w3_w"].T).reshape(32, 128, 1024)
        base[f"t{L}_s12"] = np.concatenate([fm_vec(inputs[p + "w1_s"]),
                                            fm_vec(inputs[p + "w2_s"])], axis=1)
        base[f"t{L}_sh12"] = np.concatenate([fm_vec(inputs[p + "h1_s"]),
                                             fm_vec(inputs[p + "h2_s"])], axis=1)
        base[f"t{L}_s3"] = np.ascontiguousarray(inputs[p + "w3_s"])
    if "l2f1" in steps or "l2f2" in steps:
        base["l2f_wrT"] = np.ascontiguousarray(inputs["L2_ffn_Wr"].T).reshape(8, 128, 8)
        base["l2f_w12"] = np.stack([tile_lhsT(np.ascontiguousarray(
            inputs["L2_ffn_W12"][e].T), 8, 64) for e in range(8)])
        base["l2f_w3"] = np.stack([np.ascontiguousarray(
            inputs["L2_ffn_W3"][e].T).reshape(32, 128, 1024) for e in range(8)])
        base["l2f_lng"] = np.ascontiguousarray(inputs["L2_ffn_ln_gamma"])
        base["l2f_lnb"] = np.ascontiguousarray(inputs["L2_ffn_ln_beta"])
        base["l2f_iota8"] = np.broadcast_to(
            np.arange(8, dtype=np.float32)[None, :], (128, 8)).copy()
    if "l0f1" in steps or "l0f2" in steps:
        w1s, w3s, w2s = [], [], []
        for e in range(8):
            w1s.append(tile_lhsT(np.ascontiguousarray(inputs["L0_ffn_routed_w1"][e]), 8, 4))
            w3s.append(tile_lhsT(np.ascontiguousarray(inputs["L0_ffn_routed_w3"][e]), 8, 4))
            w2s.append(np.ascontiguousarray(inputs["L0_ffn_routed_w2"][e]).reshape(4, 128, 1024))
        for e in range(2):
            w1s.append(tile_lhsT(np.ascontiguousarray(inputs["L0_ffn_shared_w1"][e]), 8, 4))
            w3s.append(tile_lhsT(np.ascontiguousarray(inputs["L0_ffn_shared_w3"][e]), 8, 4))
            w2s.append(np.ascontiguousarray(inputs["L0_ffn_shared_w2"][e]).reshape(4, 128, 1024))
        base["l0f_w1"] = np.stack(w1s)
        base["l0f_w3"] = np.stack(w3s)
        base["l0f_w2"] = np.stack(w2s)
        base["l0f_rw"] = np.ascontiguousarray(inputs["L0_ffn_router_w"]).reshape(8, 128, 8)
        base["l0f_rb"] = np.ascontiguousarray(inputs["L0_ffn_router_b"])
    if "l2a" in steps:
        base["l2_wq"] = np.ascontiguousarray(inputs["L2_attn_wq"]).reshape(8, 128, 1024)
        base["l2_wk"] = np.ascontiguousarray(inputs["L2_attn_wk"]).reshape(4, 128, 1024)
        base["l2_wv"] = np.ascontiguousarray(inputs["L2_attn_wv"]).reshape(4, 128, 1024)
        base["l2_wo"] = np.ascontiguousarray(inputs["L2_attn_wo"]).reshape(8, 128, 1024)
        for nm, wn in (("bq", "L2_attn_bq"), ("bk", "L2_attn_bk"),
                       ("bv", "L2_attn_bv"), ("bo", "L2_attn_bo"),
                       ("lng", "L2_attn_ln_g"), ("lnb", "L2_attn_ln_b")):
            base[f"l2_{nm}"] = np.ascontiguousarray(inputs[wn])
    if "l0a" in steps:
        base["l0_wqkv"] = np.ascontiguousarray(inputs["L0_attn_w_qkv"]).reshape(8, 128, 1536)
        base["l0_wo"] = np.ascontiguousarray(inputs["L0_attn_w_o"]).reshape(8, 128, 1024)
        half = 32
        inv_freq = (1.0 / 10000.0 ** (np.arange(half, dtype=np.float32) / half))
        ang = (np.arange(2048, dtype=np.float32)[:, None] * inv_freq[None, :])
        base["l0_cos"] = np.cos(ang).astype(np.float32)
        base["l0_sin"] = np.sin(ang).astype(np.float32)
        m4 = np.zeros((4, 128, 512), np.float32)
        for o in range(4):
            for p in range(128):
                m4[o, p, o * 128 + p + 1:] = -1e30
        base["l0_mask4"] = m4
    if "l1a" in steps:
        for nm, wn in (("wq", "L1_attn_Wq"), ("wk", "L1_attn_Wk"),
                       ("wv", "L1_attn_Wv"), ("wo", "L1_attn_Wo")):
            base[f"l1_{nm}"] = np.ascontiguousarray(inputs[wn]).reshape(8, 128, 1024)
        base["l1_lnw"] = np.ascontiguousarray(inputs["L1_attn_ln_w"])
        base["l1_lnb"] = np.ascontiguousarray(inputs["L1_attn_ln_b"])
        half = 512
        freq = (10000.0 ** (-np.arange(half, dtype=np.float32) / half))
        ang = np.arange(2048, dtype=np.float32)[:, None] * freq[None, :]
        base["l1_cos"] = np.cos(ang).astype(np.float32)
        base["l1_sin"] = np.sin(ang).astype(np.float32)
    if "l3a" in steps:
        W_in2 = np.ascontiguousarray(inputs["L3_attn_W_in"][1024:2048])
        base["l3_win2"] = tile_lhsT(np.ascontiguousarray(W_in2.T), 8, 8)
        base["l3_wgate"] = tile_lhsT(np.ascontiguousarray(inputs["L3_attn_W_gate"].T), 8, 8)
        base["l3_bin2"] = fm_vec(np.ascontiguousarray(inputs["L3_attn_b_in"][1024:2048]))
        base["l3_bgate"] = fm_vec(inputs["L3_attn_b_gate"])
        base["l3_A"] = np.ascontiguousarray(inputs["L3_attn_A"])
        base["l3_bmT"] = np.ascontiguousarray(
            inputs["L3_attn_Bm"].T.reshape(8, 128, 16).transpose(1, 0, 2).reshape(128, 128))
        base["l3_cmT"] = np.ascontiguousarray(inputs["L3_attn_Cm"].T)
        base["l3_woutT"] = np.ascontiguousarray(inputs["L3_attn_W_out"].T).reshape(8, 128, 1024)
        base["l3_bout"] = np.ascontiguousarray(inputs["L3_attn_b_out"])
    maps = []
    for c in range(NCORES):
        m = dict(base)
        m["x_in"] = np.ascontiguousarray(x[c])
        if "l2f1" in steps or "l2f2" in steps:
            m["l2f_cmask"] = (np.arange(8) < c).astype(np.float32)
        maps.append(m)
    return maps


def kernel(steps=None, x_override=None, **inputs):
    steps = steps or ALL_STEPS
    prog = Prog(steps)
    nc = prog.build()
    maps = host_maps(inputs, steps, x_override)
    res = run_bass_kernel_spmd(nc, maps, core_ids=list(range(NCORES)))
    return np.stack([r["x_out"] for r in res.results], axis=0)
